# version 05
# speedup vs baseline: 1.1577x; 1.0083x over previous
import os
import numpy as np
from contextlib import ExitStack

import concourse.bass as bass
import concourse.mybir as mybir
from concourse.ap import AP
from concourse.bass_utils import run_bass_kernel_spmd

F32 = mybir.dt.float32
BF16 = mybir.dt.bfloat16
AF = mybir.ActivationFunctionType
ALU = mybir.AluOpType
AX = mybir.AxisListType

B, S, D = 4, 4096, 1024
T = 2048
NT = T // 128
NH, HD = 16, 64
NEG = -1e9

STAGE = 4


STREAM_OF = {"pe": "tensor", "dve": "vector", "act": "scalar",
             "pool": "gpsimd", "sync": "sync"}
NDMASEM = 6
SEM_STEP = {"pe": 1, "dve": 1, "act": 1, "pool": 1, "cc": 1}
for _i in range(NDMASEM):
    SEM_STEP[f"sw{_i}"] = 16
    SEM_STEP[f"hw{_i}"] = 16
KIND_SEM = {"compute": None, "swdma": "sw", "hwdma": "hw", "cc": "cc"}


def _overlap(a, b):
    pa = isinstance(a, tuple) and len(a) == 3 and a[0] == "@"
    pb = isinstance(b, tuple) and len(b) == 3 and b[0] == "@"
    if pa or pb:
        if not (pa and pb):
            return True
        if a[1] != b[1]:
            return True
        a, b = a[2], b[2]
    if a is None or b is None:
        return True
    ra = isinstance(a, tuple) and len(a) and a[0] == "R"
    rb = isinstance(b, tuple) and len(b) and b[0] == "R"
    if ra and rb:
        return a[1] < b[2] and b[1] < a[2]
    if ra or rb:
        return True
    return a == b


class Op:
    __slots__ = ("stream", "kind", "fn", "reads", "writes", "idx", "sem",
                 "need_inc", "waits", "tag", "sem_after")

    def __init__(self, stream, kind, fn, reads, writes, tag):
        self.stream, self.kind, self.fn = stream, kind, fn
        self.reads, self.writes, self.tag = list(reads), list(writes), tag


class Buf:
    __slots__ = ("name", "t", "serialize", "prefix")

    def __init__(self, name, t, serialize=False, prefix=None):
        self.name, self.t, self.serialize = name, t, serialize
        self.prefix = prefix

    def __getitem__(self, key):
        return self.t[key]

    @property
    def ap(self):
        return self.t[:]


class Em:
    def __init__(self, nc):
        self.nc = nc
        self.ops = []
        self.stack = ExitStack()

    def sb(self, name, shape, dtype=F32, serialize=False):
        return Buf(name, self.stack.enter_context(
            self.nc.sbuf_tensor(name, shape, dtype)), serialize)

    def psum(self, name, shape):
        return Buf(name, self.stack.enter_context(
            self.nc.psum_tensor(name, shape, F32)), serialize=True)

    def dram(self, name, shape, dtype, kind="Internal"):
        return Buf(name, self.nc.dram_tensor(name, shape, dtype, kind=kind))

    def op(self, stream, fn, reads=(), writes=(), kind="compute", tag=""):
        def tr(lst):
            return [(b, ("@", b.prefix, t)) if b.prefix is not None else (b, t)
                    for (b, t) in lst]
        reads = tr(list(reads))
        writes = tr(list(writes))
        fcb = getattr(self, "fc_buf", None)
        if fcb is not None and stream == "act":
            reads.append((fcb, None))
        self.ops.append(Op(stream, kind, fn, reads, writes, tag))

    def ov(self, zone, prefix, cols, dtype, byteoff):
        t = self.nc.alloc_sbuf_tensor_at(
            "ov_" + prefix, [128, cols], dtype,
            offset=self.nc.lookup_mloc(zone.t).addr + byteoff)
        return Buf(zone.name, t, prefix=prefix)

    def emit(self):
        nc, ops = self.nc, self.ops
        ndma = {"swdma": 0, "hwdma": 0}
        for i, o in enumerate(ops):
            o.idx = i
            base = KIND_SEM[o.kind] or o.stream
            if o.kind in ("swdma", "hwdma"):
                base = f"{base[:2]}{ndma[o.kind] % NDMASEM}"
                ndma[o.kind] += 1
            o.sem = base
            o.need_inc = o.kind != "compute"
            o.waits = []
        writes_log, reads_log = {}, {}
        for o in ops:
            deps = set()
            for (b, tag) in o.reads:
                for (t2, j) in writes_log.get(b.name, ()):
                    if _overlap(tag, t2):
                        deps.add(j)
                if b.serialize:
                    for (t2, j) in reads_log.get(b.name, ()):
                        if _overlap(tag, t2):
                            deps.add(j)
            for (b, tag) in o.writes:
                for (t2, j) in writes_log.get(b.name, ()):
                    if _overlap(tag, t2):
                        deps.add(j)
                for (t2, j) in reads_log.get(b.name, ()):
                    if _overlap(tag, t2):
                        deps.add(j)
            for j in sorted(deps):
                p = ops[j]
                if p.stream == "pe" and o.stream == "pe":
                    continue
                o.waits.append(j)
                p.need_inc = True
            for (b, tag) in o.reads:
                reads_log.setdefault(b.name, []).append((tag, o.idx))
            for (b, tag) in o.writes:
                wl = writes_log.setdefault(b.name, [])
                wl.append((tag, o.idx))
                if len(wl) > 96:
                    seen, outl = set(), []
                    for (t2, j) in reversed(wl):
                        k2 = repr(t2)
                        if k2 in seen:
                            continue
                        seen.add(k2)
                        outl.append((t2, j))
                    writes_log[b.name] = outl[::-1]
            for (b, tag) in o.reads + o.writes:
                rl = reads_log.get(b.name)
                if rl and len(rl) > 384:
                    drop, keep = rl[:-192], rl[-192:]
                    reads_log[b.name] = [(None, max(j for _, j in drop))] + keep

        semval = {k: 0 for k in SEM_STEP}
        for o in ops:
            if o.need_inc:
                semval[o.sem] += SEM_STEP[o.sem]
                o.sem_after = semval[o.sem]
            else:
                o.sem_after = None
        self.final_semvals = dict(semval)

        sems = {k: self.stack.enter_context(nc.semaphore("S_" + k))
                for k in SEM_STEP}
        self.sems = sems
        block = self.stack.enter_context(nc.Block())
        per_stream = {s: [o for o in ops if o.stream == s] for s in STREAM_OF}

        def run_stream(eng, stream):
            clock = {k: 0 for k in SEM_STEP}
            for o in per_stream[stream]:
                for j in o.waits:
                    p = ops[j]
                    if clock[p.sem] < p.sem_after:
                        eng.wait_ge(sems[p.sem], p.sem_after)
                        clock[p.sem] = p.sem_after
                if o.need_inc:
                    prev = o.sem_after - SEM_STEP[o.sem]
                    if prev > 0 and clock[o.sem] < prev:
                        eng.wait_ge(sems[o.sem], prev)
                        clock[o.sem] = prev
                ins = o.fn(eng)
                if o.need_inc:
                    assert ins is not None, f"op '{o.tag}' returned None"
                    if o.kind == "cc":
                        ins.then_inc(sems[o.sem])
                    else:
                        ins.then_inc(sems[o.sem], SEM_STEP[o.sem])
            if stream == "sync":
                for k, v in self.final_semvals.items():
                    if v and clock[k] < v:
                        eng.wait_ge(sems[k], v)

        for stream, section in STREAM_OF.items():
            getattr(block, section)(lambda eng, s=stream: run_stream(eng, s))



def build_program(stage=STAGE):
    nc = bass.Bass("TRN2", target_bir_lowering=False, debug=False)
    em = Em(nc)
    PAIRS = [[0, 1], [2, 3], [4, 5], [6, 7]]

    def dinp(name, shape, dtype=F32):
        return em.dram(name, shape, dtype, kind="ExternalInput")

    x0 = dinp("x0", [T, D])
    wq_a = dinp("wq_a", [D, D]); wk_a = dinp("wk_a", [D, D])
    wv_a = dinp("wv_a", [D, D]); wo_a = dinp("wo_a", [D, D])
    ident_in = dinp("ident", [128, 128])
    identf_in = dinp("identf", [128, 128])
    mska_in = dinp("mska", [16, 128, 512])
    mskb_in = dinp("mskb", [16, 128, 512])
    wcol_in = dinp("wcol", [128, 16])
    selp_in = dinp("selp", [8, 16, 128])
    tbl_in = dinp("tbl", [4, T, 32])
    shu_in = dinp("shu", [128, 128])
    shd_in = dinp("shd", [128, 128])
    e127_in = dinp("e127", [128, 1])
    sel64_in = dinp("sel64", [128, 64])
    tria_in = dinp("tria", [128, 128])
    tric_in = dinp("tric", [128, 128])
    lma_in = dinp("lma", [128, 128])
    lm0b_in = dinp("lm0b", [128, 128])
    hsc_in = dinp("hsc", [1, 1])
    msks_in = dinp("msks", [128, 1152])
    f_wqkv = dinp("f_wqkv", [6, D, 3072])
    f_qs = dinp("f_qs", [6, 64])
    f_ks = dinp("f_ks", [6, 64])
    f_wo = dinp("f_wo", [6, D, D])
    f_lng = dinp("f_lng", [6, 128, 8])
    f_lnb = dinp("f_lnb", [6, 128, 8])
    f_w1 = dinp("f_w1", [6, D, 5460])
    f_b1 = dinp("f_b1", [6, 128, 44])
    f_w2 = dinp("f_w2", [6, 2730, D])
    f_b2 = dinp("f_b2", [6, 128, 8])
    out = em.dram("out", [T, D], F32, kind="ExternalOutput")

    kcc = em.dram("kcc", [D, T], BF16)
    kg = em.dram("kg", [2 * D, T], BF16)
    vcc = em.dram("vcc", [T, D], BF16)
    vg = em.dram("vg", [2 * T, D], BF16)
    bndx = em.dram("bndx", [1, D], F32)
    bndxg = em.dram("bndxg", [2, D], F32)
    kb = em.dram("kb", [D, 256], BF16)
    kbg = em.dram("kbg", [2 * D, 256], BF16)
    vb = em.dram("vb", [256, D], BF16)
    vbg = em.dram("vbg", [512, D], BF16)
    kst_d = em.dram("kst_d", [D, 256], BF16)
    qst_d = em.dram("qst_d", [128, 2048], BF16)
    vst_d = em.dram("vst_d", [256, D], BF16)
    xo_spill = em.dram("xo_spill", [T, D], F32)
    wq_b = dinp("wq_b", [D, D]); wk_b = dinp("wk_b", [D, D])
    wv_b = dinp("wv_b", [D, D]); wo_b = dinp("wo_b", [D, D])
    wdn_b = dinp("wdn_b", [D, 256])
    wupk_b = dinp("wupk_b", [256, D]); wupv_b = dinp("wupv_b", [256, D])
    dsab_in = dinp("dsab", [128, 256])
    dcr_in = dinp("dcr", [1, 1280])
    latc = em.dram("latc", [256, T], BF16)
    latg = em.dram("latg", [512, T], BF16)
    kb_d = em.dram("kb_d", [D, 2 * T], BF16)
    vb_d = em.dram("vb_d", [2 * T, D], BF16)

    CW = 2050
    XO = em.sb("XO", [128, NT * 1024], F32)
    Z1 = em.sb("Z1", [128, 24576], BF16)
    Z2 = em.sb("Z2", [128, 16384], BF16)
    Z3 = em.sb("Z3", [128, 8192], BF16)
    Z4 = em.sb("Z4", [128, 8192], BF16)
    TMPz = em.sb("TMPz", [128, 4096], BF16)
    TMP = Buf("TMPz", TMPz.t, prefix="TM")
    QF = em.ov(Z1, "QF", 16384, BF16, 0)
    WQKV = em.ov(Z1, "WQ", 24576, BF16, 0)
    H2F = WQKV
    KCH = em.ov(Z1, "KC", 8192, BF16, 32768)
    WT = KCH
    XF = em.ov(Z2, "XF", 16384, BF16, 0)
    AOF = XF
    KRN = em.ov(Z2, "KR", 6144, BF16, 0)
    VRN = em.ov(Z2, "VR", 6144, BF16, 12288)
    XST = em.ov(Z2, "X2", 2056, BF16, 24576)
    PCHS = em.ov(Z2, "PC", 768, BF16, 28704)
    PTCS = em.ov(Z2, "PT", 768, BF16, 30240)
    XNFG = em.ov(Z2, "XN", 8196, BF16, 0)
    W1T = em.ov(Z2, "W1", 4096, BF16, 16416)
    W2T = em.ov(Z2, "W2", 2816, BF16, 24608)
    WTT = em.ov(Z3, "WTT", 8192, BF16, 0)
    W2TB = em.ov(Z3, "W2B", 2816, BF16, 0)
    WOF = em.ov(Z3, "WO", 8192, BF16, 0)
    VCH = em.ov(Z4, "VC", 8192, BF16, 0)
    QFT = em.ov(Z4, "QT", 2048, BF16, 0)
    QKT = em.ov(Z4, "QK", 2048, BF16, 4096)
    QR = em.ov(Z4, "QR", 1024, BF16, 8192)
    KQR = em.ov(Z4, "KQ", 1024, BF16, 10240)
    OE = em.ov(Z4, "OE", 1024, BF16, 12288)
    OUT2F = em.ov(Z4, "O2", 8192, BF16, 0)
    WQN = em.ov(Z3, "BWQ", 8192, BF16, 0)
    WON = em.ov(Z3, "BWO", 8192, BF16, 0)
    WUPK = em.ov(Z3, "BUK", 2048, BF16, 0)
    WUPV = em.ov(Z3, "BUV", 2048, BF16, 4096)
    WVN = em.ov(Z4, "BWV", 8192, BF16, 0)
    LATT = em.ov(Z4, "BLT", 4096, BF16, 0)
    LATG = em.ov(Z4, "BLG", 8192, BF16, 0)
    WDWN = em.ov(TMPz, "BDW", 2048, BF16, 0)
    WDE2 = em.ov(TMPz, "BDE", 2048, BF16, 4096)
    DSAB = em.ov(TMPz, "BDS", 512, BF16, 0)
    DCRB = em.ov(TMPz, "BDc", 1280, BF16, 1024)
    KST = em.ov(TMPz, "KS", 2048, BF16, 0)
    VST = em.ov(TMPz, "VS", 1024, BF16, 4096)
    PCH = em.ov(TMPz, "PCA", 1024, BF16, 4096)
    PTC = em.ov(TMPz, "PTA", 1024, BF16, 6144)
    MSKG = em.sb("MSKG", [128, 4 * 512], BF16)
    IDN = em.sb("IDN", [128, 128], BF16)
    IDNF = em.sb("IDNF", [128, 128], F32)
    WCOL = em.sb("WCOL", [128, 16], F32)
    SELP = em.sb("SELP", [16, 8 * 128], BF16)
    RROWB = em.sb("RROWB", [16, 2 * 128], BF16)
    STAT = em.sb("STAT", [128, 128], F32)
    LRED = em.sb("LRED", [128, 64], F32)
    RB = em.sb("RB", [128, 2 * 128], F32)
    TBL = em.sb("TBL", [128, 4 * 512], BF16)
    QSB = em.sb("QSB", [128, 128], BF16)
    GBC = em.sb("GBC", [128, 16], F32)
    B1C = em.sb("B1C", [128, 44], F32)
    B2C = em.sb("B2C", [128, 8], F32)
    LAT = em.sb("LAT", [128, 256], F32)
    LPQ = Buf("LAT", LAT.t)
    HB = em.sb("HB", [1, 1024], BF16)
    HBB = em.sb("HBB", [1, 1024], BF16)
    HS = em.sb("HS", [1, 8], F32)
    QS1 = em.sb("QS1", [1, 128], F32)
    SHUF = em.sb("SHUF", [128, 128], F32)
    SHDF = em.sb("SHDF", [128, 128], F32)
    E127 = em.sb("E127", [128, 1], F32)
    SEL64 = em.sb("SEL64", [128, 64], F32)
    ONEF = em.sb("ONEF", [1, 128], F32)
    HSC = em.sb("HSC", [1, 1], F32)
    TRIA = em.sb("TRIA", [128, 128], BF16)
    TRIC = em.sb("TRIC", [128, 128], BF16)
    LMA = em.sb("LMA", [128, 128], BF16)
    LM0B = em.sb("LM0B", [128, 128], BF16)
    XCR = em.sb("XCR", [128, 8], BF16)

    ZROW = em.sb("ZROW", [1, 512], BF16)
    em.op("dve", lambda ve: ve.memset(ZROW.ap, 0.0), [], [(ZROW, None)], tag="zrow")
    FC = em.sb("FC", [128, 12], F32)
    FCONSTS = [1e-5, -0.5, 0.125, 8.0, 0.5, 2.0, -1.0, 0.0, -2.0794415416798357, 1.25e-13, 1e-12, 1.0]
    for i, v in enumerate(FCONSTS):
        nc.const_aps.register(F32, v, FC[:, i:i + 1])
        em.op("dve", lambda ve, i=i, v=v: ve.memset(FC[:, i:i + 1], v),
              [], [(FC, ("c", i))], tag=f"fc{i}")
    em.fc_buf = FC

    PS_A = em.psum("PS_A", [128, 512])
    PS_B = em.psum("PS_B", [128, 512])
    PS_T = em.psum("PS_T", [128, 512])
    PS_O = em.psum("PS_O", [128, 2048])
    PS_DEC = em.psum("PS_DEC", [128, 512])

    def swload(dst_ap, src_ap, reads, writes, tag=""):
        em.op("pool", lambda g, d=dst_ap, s=src_ap: g.dma_start(out=d, in_=s),
              reads, writes, kind="swdma", tag=tag)

    def hwload(dst_ap, src_ap, reads, writes, tag=""):
        em.op("sync", lambda e, d=dst_ap, s=src_ap: e.dma_start(out=d, in_=s),
              reads, writes, kind="hwdma", tag=tag)

    hwload(XO.ap.rearrange("p (t d) -> p t d", d=1024),
           x0.t.ap().rearrange("(t p) d -> p t d", p=128),
           [(x0, None)], [(XO, None)], "ld_x0")
    swload(IDN.ap, ident_in.t.ap(), [(ident_in, None)], [(IDN, None)], "ld_idn")
    hwload(IDNF.ap, identf_in.t.ap(), [(identf_in, None)], [(IDNF, None)], "ld_idnf")
    hwload(WCOL.ap, wcol_in.t.ap(), [(wcol_in, None)], [(WCOL, None)], "ld_wcol")
    swload(SELP.ap.rearrange("p (s f) -> p s f", f=128),
           selp_in.t.ap().rearrange("s p f -> p s f"),
           [(selp_in, None)], [(SELP, None)], "ld_selp")
    swload(TBL.ap.rearrange("p (b t j) -> p b t j", b=4, t=16),
           tbl_in.t.ap().rearrange("b (t p) j -> p b t j", p=128),
           [(tbl_in, None)], [(TBL, None)], "ld_tbl")
    hwload(SHUF.ap, shu_in.t.ap(), [(shu_in, None)], [(SHUF, None)], "ld_shu")
    hwload(SHDF.ap, shd_in.t.ap(), [(shd_in, None)], [(SHDF, None)], "ld_shd")
    hwload(E127.ap, e127_in.t.ap(), [(e127_in, None)], [(E127, None)], "ld_e127")
    hwload(SEL64.ap, sel64_in.t.ap(), [(sel64_in, None)], [(SEL64, None)], "ld_sel64")
    swload(TRIA.ap, tria_in.t.ap(), [(tria_in, None)], [(TRIA, None)], "ld_tria")
    swload(TRIC.ap, tric_in.t.ap(), [(tric_in, None)], [(TRIC, None)], "ld_tric")
    swload(LMA.ap, lma_in.t.ap(), [(lma_in, None)], [(LMA, None)], "ld_lma")
    swload(LM0B.ap, lm0b_in.t.ap(), [(lm0b_in, None)], [(LM0B, None)], "ld_lm0b")
    hwload(HSC.ap, hsc_in.t.ap(), [(hsc_in, None)], [(HSC, None)], "ld_hsc")
    em.op("dve", lambda v: v.memset(ONEF.ap, 1.0), [], [(ONEF, None)], tag="onef")

    def R(lo, hi):
        return ("R", lo, hi)

    def emit_outer_ln(uid):
        MUc, E2c, RSc, NMRc, SCRc, VARc = 32, 48, 64, 80, 96, 112
        for t in range(NT):
            xt = XO[:, 1024 * t:1024 * (t + 1)]
            em.op("dve", lambda v, xt=xt, t=t: v.reduce_sum(
                STAT[:, t:t + 1], xt, axis=AX.X),
                [(XO, ("t", t))], [(STAT, R(t, t + 1))], tag=f"ln{uid}_sum{t}")
            em.op("act", lambda a, xt=xt, t=t: a.activation(
                TMP[:, 2048:3072], xt, AF.Square,
                accum_out=STAT[:, 16 + t:17 + t]),
                [(XO, ("t", t))], [(TMP, R(2048, 3072)), (STAT, R(16 + t, 17 + t))],
                tag=f"ln{uid}_sq{t}")
        MU, E2 = STAT[:, MUc:MUc + 16], STAT[:, E2c:E2c + 16]
        RS, NMR = STAT[:, RSc:RSc + 16], STAT[:, NMRc:NMRc + 16]
        SCR, VAR = STAT[:, SCRc:SCRc + 16], STAT[:, VARc:VARc + 16]
        em.op("dve", lambda v: v.tensor_scalar_mul(MU, STAT[:, 0:16], 1.0 / D),
              [(STAT, R(0, 16))], [(STAT, R(MUc, MUc + 16))], tag=f"ln{uid}_mu")
        em.op("dve", lambda v: v.tensor_scalar_mul(E2, STAT[:, 16:32], 1.0 / D),
              [(STAT, R(16, 32))], [(STAT, R(E2c, E2c + 16))], tag=f"ln{uid}_e2")
        em.op("dve", lambda v: v.tensor_mul(SCR, MU, MU),
              [(STAT, R(MUc, MUc + 16))], [(STAT, R(SCRc, SCRc + 16))],
              tag=f"ln{uid}_musq")
        em.op("dve", lambda v: v.tensor_sub(VAR, E2, SCR),
              [(STAT, R(E2c, E2c + 16)), (STAT, R(SCRc, SCRc + 16))],
              [(STAT, R(VARc, VARc + 16))], tag=f"ln{uid}_var")
        em.op("act", lambda a: a.activation(SCR, VAR, AF.Ln, bias=1e-5),
              [(STAT, R(VARc, VARc + 16)), (FC, None)],
              [(STAT, R(SCRc, SCRc + 16))], tag=f"ln{uid}_ln")
        em.op("act", lambda a: a.activation(RS, SCR, AF.Exp, scale=-0.5),
              [(STAT, R(SCRc, SCRc + 16)), (FC, None)],
              [(STAT, R(RSc, RSc + 16))], tag=f"ln{uid}_rs")
        em.op("dve", lambda v: v.scalar_tensor_tensor(
            NMR, MU, -1.0, RS, ALU.mult, ALU.mult),
            [(STAT, R(MUc, MUc + 16)), (STAT, R(RSc, RSc + 16))],
            [(STAT, R(NMRc, NMRc + 16))], tag=f"ln{uid}_nmr")
        for t in range(NT):
            slot = t % 2
            xt = XO[:, 1024 * t:1024 * (t + 1)]
            tmp = TMP[:, 1024 * slot:1024 * (slot + 1)]
            em.op("act", lambda a, xt=xt, tmp=tmp, t=t: a.activation(
                tmp, xt, AF.Identity, scale=RS[:, t:t + 1], bias=NMR[:, t:t + 1]),
                [(XO, ("t", t)), (STAT, R(RSc, NMRc + 16))],
                [(TMP, ("s", slot))], tag=f"ln{uid}_app{t}")
            for c in range(8):
                pb = PS_T if c % 2 == 0 else PS_B
                po = 0 if c % 2 == 0 else 0
                em.op("pe", lambda te, tmp=tmp, c=c, pb=pb: te.matmul(
                    pb[:, 0:128], tmp[:, 128 * c:128 * (c + 1)], IDN.ap,
                    start=True, stop=True),
                    [(TMP, ("s", slot)), (IDN, None)], [(pb, R(0, 128))],
                    tag=f"ln{uid}_T{t}_{c}")
                em.op("dve", lambda v, pb=pb, c=c, t=t: v.tensor_copy(
                    XF[:, 2048 * c + 128 * t:2048 * c + 128 * (t + 1)], pb[:, 0:128]),
                    [(pb, R(0, 128))], [(XF, ("c", c, "t", t))],
                    tag=f"ln{uid}_Te{t}_{c}")

    def emit_weight_T(wsrc, uid):
        swload(WT.ap.rearrange("p (o d) -> p o d", d=1024),
               wsrc.t.ap().rearrange("(o p) d -> p o d", p=128),
               [(wsrc, None)], [(WT, None)], f"w{uid}_ld")
        for dc in range(8):
            for ot in range(8):
                pb = PS_T if (dc * 8 + ot) % 2 == 0 else PS_B
                em.op("pe", lambda te, dc=dc, ot=ot, pb=pb: te.matmul(
                    pb[:, 0:128],
                    WT[:, 1024 * ot + 128 * dc:1024 * ot + 128 * (dc + 1)],
                    IDN.ap, start=True, stop=True),
                    [(WT, None), (IDN, None)], [(pb, R(0, 128))],
                    tag=f"w{uid}_T{dc}_{ot}")
                em.op("dve", lambda v, dc=dc, ot=ot, pb=pb: v.tensor_copy(
                    WTT[:, 1024 * dc + 128 * ot:1024 * dc + 128 * (ot + 1)],
                    pb[:, 0:128]),
                    [(pb, R(0, 128))], [(WTT, R(1024 * dc + 128 * ot, 1024 * dc + 128 * (ot + 1)))],
                    tag=f"w{uid}_Te{dc}_{ot}")

    def wtt_tag(uid):
        return [(WTT, (uid, dc, ot)) for dc in range(8) for ot in range(8)]

    def emit_proj_fm(uid, dst, dst_tagfn, wbuf=None):
        wb = WTT if wbuf is None else wbuf
        for ot in range(8):
            for tc in range(4):
                pb = PS_A if tc % 2 == 0 else PS_B
                for dc in range(8):
                    wrd = ((wb, R(1024 * dc + 128 * ot, 1024 * dc + 128 * (ot + 1)))
                           if wbuf is None else (wb, None))
                    em.op("pe", lambda te, ot=ot, tc=tc, dc=dc, pb=pb, wb=wb: te.matmul(
                        pb[:, 0:512],
                        wb[:, 1024 * dc + 128 * ot:1024 * dc + 128 * (ot + 1)],
                        XF[:, 2048 * dc + 512 * tc:2048 * dc + 512 * (tc + 1)],
                        start=(dc == 0), stop=(dc == 7)),
                        [wrd, (XF, None)], [(pb, R(0, 512))],
                        tag=f"{uid}_mm{ot}_{tc}_{dc}")
                dtag = dst_tagfn(ot, tc)
                em.op("dve", lambda v, ot=ot, tc=tc, pb=pb, d=dst: v.tensor_copy(
                    d[:, 2048 * ot + 512 * tc:2048 * ot + 512 * (tc + 1)],
                    pb[:, 0:512]),
                    [(pb, R(0, 512))], [(dst, dtag)], tag=f"{uid}_ev{ot}_{tc}")

    def emit_attn_a(hf):
        emit_outer_ln("a")
        emit_weight_T(wq_a, "q")
        emit_proj_fm("q", QF, lambda ot, tc: ("q", ot, tc))
        emit_weight_T(wk_a, "k")
        for ot in range(8):
            for tc in range(4):
                pb = PS_A if tc % 2 == 0 else PS_B
                for dc in range(8):
                    em.op("pe", lambda te, ot=ot, tc=tc, dc=dc, pb=pb: te.matmul(
                        pb[:, 0:512],
                        WTT[:, 1024 * dc + 128 * ot:1024 * dc + 128 * (ot + 1)],
                        XF[:, 2048 * dc + 512 * tc:2048 * dc + 512 * (tc + 1)],
                        start=(dc == 0), stop=(dc == 7)),
                        [(WTT, R(1024 * dc + 128 * ot, 1024 * dc + 128 * (ot + 1))), (XF, None)], [(pb, R(0, 512))],
                        tag=f"k_mm{ot}_{tc}_{dc}")
                em.op("dve", lambda v, tc=tc, pb=pb: v.tensor_copy(
                    KST[:, 512 * tc:512 * (tc + 1)], pb[:, 0:512]),
                    [(pb, R(0, 512))], [(KST, R(512 * tc, 512 * (tc + 1)))],
                    tag=f"k_ev{ot}_{tc}")
            hwload(kcc.t[128 * ot:128 * (ot + 1), :], KST[:, 0:2048],
                   [(KST, R(0, 2048))], [(kcc, ("ot", ot))], f"k_st{ot}")
        emit_weight_T(wv_a, "v")
        for t in range(NT):
            pb2 = [PS_A, PS_B]
            for n in range(2):
                pb = pb2[n]
                for dc in range(8):
                    em.op("pe", lambda te, t=t, n=n, dc=dc, pb=pb: te.matmul(
                        pb[:, 0:512],
                        XF[:, 2048 * dc + 128 * t:2048 * dc + 128 * (t + 1)],
                        WTT[:, 1024 * dc + 512 * n:1024 * dc + 512 * (n + 1)],
                        start=(dc == 0), stop=(dc == 7)),
                        [(WTT, R(1024 * dc, 1024 * (dc + 1))), (XF, None)], [(pb, R(0, 512))],
                        tag=f"v_mm{t}_{n}_{dc}")
                em.op("dve", lambda v, n=n, pb=pb: v.tensor_copy(
                    VST[:, 512 * n:512 * (n + 1)], pb[:, 0:512]),
                    [(pb, R(0, 512))], [(VST, R(512 * n, 512 * (n + 1)))],
                    tag=f"v_ev{t}_{n}")
            hwload(vcc.t[128 * t:128 * (t + 1), :], VST[:, 0:1024],
                   [(VST, R(0, 1024))], [(vcc, ("t", t))], f"v_st{t}")
        em.op("pool", lambda g: g.collective_compute(
            "AllGather", ALU.bypass, replica_groups=PAIRS,
            ins=[kcc.t.ap().opt()], outs=[kg.t.ap().opt()]),
            [(kcc, None)], [(kg, None)], kind="cc", tag="ccK")
        em.op("pool", lambda g: g.collective_compute(
            "AllGather", ALU.bypass, replica_groups=PAIRS,
            ins=[vcc.t.ap().opt()], outs=[vg.t.ap().opt()]),
            [(vcc, None)], [(vg, None)], kind="cc", tag="ccV")
        emit_weight_T(wo_a, "o")

        vg3 = vg.t.ap().rearrange("(blk p) d -> p blk d", p=128)
        for grp in range(8):
            qts = [2 * grp, 2 * grp + 1]
            cmax = 4 + qts[1] // 4
            for bk in range(4):
                em.op("pe", lambda te, bk=bk: te.matmul(
                    PS_O[:, 512 * bk:512 * (bk + 1)], ZROW[0:1, 0:128],
                    ZROW[0:1, 0:512], start=True, stop=False,
                    skip_group_check=True),
                    [(ZROW, None)],
                    [(PS_O, ("oc", 128 * (4 * bk))), (PS_O, ("oc", 128 * (4 * bk + 1))),
                     (PS_O, ("oc", 128 * (4 * bk + 2))), (PS_O, ("oc", 128 * (4 * bk + 3)))],
                    tag=f"a_oz{grp}_{bk}")
            for qi, qt in enumerate(qts):
                swload(MSKG[:, 512 * qi:512 * (qi + 1)],
                       mska_in.t.ap()[qt], [(mska_in, None)],
                       [(MSKG, ("m", qi))], f"a_mld{grp}_{qi}")
                swload(MSKG[:, 1024 + 512 * qi:1024 + 512 * (qi + 1)],
                       mskb_in.t.ap()[qt], [(mskb_in, None)],
                       [(MSKG, ("m", 2 + qi))], f"a_mldb{grp}_{qi}")
            for ci in range(cmax + 1):
                half, loc = ci // 4, ci % 4
                slot = ci % 2
                hwload(
                    KCH.ap.rearrange("p (s o c) -> p s o c", s=2, c=512)[:, slot],
                    kg.t.ap().rearrange("(hh o p) c -> p hh o c", p=128, o=8)
                    [:, half, :, 512 * loc:512 * (loc + 1)],
                    [(kg, None)], [(KCH, ("s", slot))], f"a_kch{grp}_{ci}")
                hwload(
                    VCH.ap.rearrange("p (s u d) -> p s u d", s=2, d=1024)[:, slot],
                    vg3[:, 16 * half + 4 * loc:16 * half + 4 * (loc + 1)],
                    [(vg, None)], [(VCH, ("vs", slot))], f"a_vch{grp}_{ci}")
                for qi, qt in enumerate(qts):
                    if ci > 4 + qt // 4:
                        continue
                    mask = None
                    if ci == qt // 4:
                        mask = MSKG[:, 512 * qi:512 * (qi + 1)]
                        mtag = ("m", qi)
                    elif ci == qt // 4 + 4:
                        mask = MSKG[:, 1024 + 512 * qi:1024 + 512 * (qi + 1)]
                        mtag = ("m", 2 + qi)
                    mid = (qt // 4 < ci < qt // 4 + 4)

                    def a_front(hh, mask=mask, mtag=mtag, mid=mid, qt=qt,
                                qi=qi, ci=ci, slot=slot):
                        ot, hi = hh // 2, hh % 2
                        spb = PS_A if (hh + qi) % 2 == 0 else PS_B
                        if mask is not None:
                            em.op("pe", lambda te, mk=mask, spb=spb:
                                  te.matmul(spb[:, 0:512], IDN.ap, mk,
                                            start=True, stop=False),
                                  [(IDN, None), (MSKG, mtag)],
                                  [(spb, R(0, 512))],
                                  tag=f"a_inj{grp}_{ci}_{qt}_{hh}")
                        qsl = QF.t[64 * hi:64 * (hi + 1),
                                   2048 * ot + 128 * qt:2048 * ot + 128 * (qt + 1)]
                        ksl = KCH.t[64 * hi:64 * (hi + 1),
                                    4096 * slot + 512 * ot:4096 * slot + 512 * (ot + 1)]
                        em.op("pe", lambda te, qsl=qsl, ksl=ksl, spb=spb,
                              first=(mask is None): te.matmul(
                            spb[:, 0:512], qsl, ksl, start=first, stop=True),
                            [(QF, None), (KCH, ("s", slot))],
                            [(spb, R(0, 512))], tag=f"a_qk{grp}_{ci}_{qt}_{hh}")
                        pslot = (qi * NH + hh) % 2
                        pdst = PCH[:, 512 * pslot:512 * (pslot + 1)]
                        lcol = 128 * qi + 8 * hh + ci
                        bias = WCOL[:, qt:qt + 1] if mid else 0.0
                        rd = [(spb, R(0, 512)), (FC, None)]
                        if mid:
                            rd.append((WCOL, None))
                        em.op("act", lambda a, spb=spb, pdst=pdst, lcol=lcol,
                              bias=bias: a.activation(
                            pdst, spb[:, 0:512], AF.Exp, bias=bias, scale=0.125,
                            accum_out=LPQ[:, lcol:lcol + 1]),
                            rd, [(PCH, ("s", pslot)), (LPQ, R(lcol, lcol + 1))],
                            tag=f"a_exp{grp}_{ci}_{qt}_{hh}")

                    def a_back(hh, qt=qt, qi=qi, ci=ci, slot=slot):
                        pslot = (qi * NH + hh) % 2
                        pdst = PCH[:, 512 * pslot:512 * (pslot + 1)]
                        for c2 in range(4):
                            em.op("pe", lambda te, pdst=pdst, c2=c2: te.matmul(
                                PS_T[:, 128 * c2:128 * (c2 + 1)],
                                pdst[:, 128 * c2:128 * (c2 + 1)], IDN.ap,
                                start=True, stop=True),
                                [(PCH, ("s", pslot)), (IDN, None)],
                                [(PS_T, R(128 * c2, 128 * (c2 + 1)))],
                                tag=f"a_T{grp}_{ci}_{qt}_{hh}_{c2}")
                        em.op("dve", lambda v, pslot=pslot: v.tensor_copy(
                            PTC[:, 512 * pslot:512 * (pslot + 1)], PS_T[:, 0:512]),
                            [(PS_T, R(0, 512))], [(PTC, ("s", pslot))],
                            tag=f"a_Te{grp}_{ci}_{qt}_{hh}")
                        pr = hh // 2
                        orow = 64 * (hh % 2)
                        ocol = 1024 * qi + 128 * pr
                        for u in range(4):
                            vsl = VCH.t[:, 4096 * slot + 1024 * u + 64 * hh:
                                        4096 * slot + 1024 * u + 64 * (hh + 1)]
                            ptsl = PTC.t[:, 512 * pslot + 128 * u:
                                         512 * pslot + 128 * (u + 1)]
                            em.op("pe", lambda te, vsl=vsl, ptsl=ptsl,
                                  orow=orow, ocol=ocol,
                                  sp=(ci == 4 + qt // 4 and u == 3): te.matmul(
                                PS_O[orow:orow + 64, ocol:ocol + 128],
                                vsl, ptsl, start=False, stop=sp,
                                tile_position=(0, orow), skip_group_check=True),
                                [(VCH, ("vs", slot)), (PTC, ("s", pslot))],
                                [(PS_O, ("oc", ocol))],
                                tag=f"a_pv{grp}_{ci}_{qt}_{hh}_{u}")

                    a_front(0)
                    for hh in range(1, NH):
                        a_front(hh)
                        a_back(hh - 1)
                    a_back(NH - 1)
            for qi, qt in enumerate(qts):
                nch = 4 + qt // 4 + 1
                em.op("dve", lambda v, qi=qi, nch=nch: v.reduce_sum(
                    LRED[:, 16 * qi:16 * (qi + 1)].unsqueeze(2),
                    LPQ.ap.rearrange("p (q h c) -> p q h c", q=2, h=16)
                    [:, qi, :, 0:nch], axis=AX.X),
                    [(LPQ, R(128 * qi, 128 * (qi + 1)))],
                    [(LRED, R(16 * qi, 16 * qi + 16))], tag=f"a_lr{grp}_{qi}")
                em.op("dve", lambda v, qi=qi: v.reciprocal(
                    LRED[:, 32 + 16 * qi:32 + 16 * (qi + 1)],
                    LRED[:, 16 * qi:16 * (qi + 1)]),
                    [(LRED, R(16 * qi, 16 * qi + 16))],
                    [(LRED, R(32 + 16 * qi, 48 + 16 * qi))], tag=f"a_rc{grp}_{qi}")
                em.op("pe", lambda te, qi=qi: te.matmul(
                    PS_T[0:16, 0:128],
                    LRED[:, 32 + 16 * qi:32 + 16 * (qi + 1)], IDNF.ap,
                    start=True, stop=True),
                    [(LRED, R(32 + 16 * qi, 48 + 16 * qi)), (IDNF, None)],
                    [(PS_T, R(0, 128))], tag=f"a_rT{grp}_{qi}")
                em.op("dve", lambda v, qi=qi: v.tensor_copy(
                    RROWB[:, 128 * qi:128 * (qi + 1)], PS_T[0:16, 0:128]),
                    [(PS_T, R(0, 128))], [(RROWB, ("q", qi))],
                    tag=f"a_rTe{grp}_{qi}")
                for pr in range(8):
                    em.op("pe", lambda te, qi=qi, pr=pr: te.matmul(
                        PS_T[:, 128:256],
                        SELP[:, 128 * pr:128 * (pr + 1)],
                        RROWB[:, 128 * qi:128 * (qi + 1)],
                        start=True, stop=True),
                        [(SELP, None), (RROWB, ("q", qi))],
                        [(PS_T, R(128, 256))], tag=f"a_rb{grp}_{qi}_{pr}")
                    em.op("dve", lambda v, qi=qi: v.tensor_copy(
                        RB[:, 128 * qi:128 * (qi + 1)], PS_T[:, 128:256]),
                        [(PS_T, R(128, 256))], [(RB, ("q", qi))],
                        tag=f"a_rbe{grp}_{qi}_{pr}")
                    ocol = 1024 * qi + 128 * pr
                    em.op("dve", lambda v, qi=qi, pr=pr, qt=qt, ocol=ocol:
                          v.tensor_mul(
                        AOF[:, 2048 * pr + 128 * qt:2048 * pr + 128 * (qt + 1)],
                        PS_O[:, ocol:ocol + 128], RB[:, 128 * qi:128 * (qi + 1)]),
                        [(PS_O, ("oc", ocol)), (RB, ("q", qi))],
                        [(AOF, ("p", pr, "t", qt))], tag=f"a_on{grp}_{qi}_{pr}")
        for t in range(NT):
            for n in range(2):
                pb = PS_A if n == 0 else PS_B
                for dc in range(8):
                    em.op("pe", lambda te, t=t, n=n, dc=dc, pb=pb: te.matmul(
                        pb[:, 0:512],
                        AOF[:, 2048 * dc + 128 * t:2048 * dc + 128 * (t + 1)],
                        WTT[:, 1024 * dc + 512 * n:1024 * dc + 512 * (n + 1)],
                        start=(dc == 0), stop=(dc == 7)),
                        [(AOF, None), (WTT, R(1024 * dc, 1024 * (dc + 1)))],
                        [(pb, R(0, 512))], tag=f"ao_mm{t}_{n}_{dc}")
                em.op("dve", lambda v, t=t, n=n, pb=pb: v.tensor_add(
                    XO[:, 1024 * t + 512 * n:1024 * t + 512 * (n + 1)],
                    XO[:, 1024 * t + 512 * n:1024 * t + 512 * (n + 1)],
                    pb[:, 0:512]),
                    [(pb, R(0, 512)), (XO, ("t", t))], [(XO, ("t", t))],
                    tag=f"ao_res{t}_{n}")


    def emit_attn_b():
        emit_outer_ln("b")
        swload(WT.ap.rearrange("p (o d) -> p o d", d=1024),
               wk_b.t.ap().rearrange("(o p) d -> p o d", p=128),
               [(wk_b, None), (WT, None)], [(WT, None)], "b_wkld")
        swload(WVN.ap.rearrange("p (o d) -> p o d", d=1024),
               wv_b.t.ap().rearrange("(o p) d -> p o d", p=128),
               [(wv_b, None), (WVN, None)], [(WVN, None)], "b_wvld")
        for h2 in range(2):
            em.op("dve", lambda v, h2=h2: v.tensor_add(
                WT[:, 4096 * h2:4096 * (h2 + 1)],
                WT[:, 4096 * h2:4096 * (h2 + 1)],
                WVN[:, 4096 * h2:4096 * (h2 + 1)]),
                [(WT, None), (WVN, None)], [(WT, None)], tag=f"b_wkv{h2}")
        for dc in range(8):
            for ot in range(8):
                pb = PS_T if (dc * 8 + ot) % 2 == 0 else PS_B
                em.op("pe", lambda te, dc=dc, ot=ot, pb=pb: te.matmul(
                    pb[:, 0:128],
                    WT[:, 1024 * ot + 128 * dc:1024 * ot + 128 * (dc + 1)],
                    IDN.ap, start=True, stop=True),
                    [(WT, None), (IDN, None)], [(pb, R(0, 128))],
                    tag=f"b_wT{dc}_{ot}")
                em.op("dve", lambda v, dc=dc, ot=ot, pb=pb: v.tensor_copy(
                    WTT[:, 1024 * dc + 128 * ot:1024 * dc + 128 * (ot + 1)],
                    pb[:, 0:128]),
                    [(pb, R(0, 128))],
                    [(WTT, R(1024 * dc + 128 * ot, 1024 * dc + 128 * (ot + 1)))],
                    tag=f"b_wTe{dc}_{ot}")
        swload(WDWN.ap.rearrange("p (mc r) -> p mc r", r=256),
               wdn_b.t.ap().rearrange("(mc p) r -> p mc r", p=128),
               [(wdn_b, None), (WDWN, None)], [(WDWN, None)], "b_wdld")
        for ic in range(8):
            pb = PS_A if ic % 2 == 0 else PS_B
            for mc in range(8):
                em.op("pe", lambda te, ic=ic, mc=mc, pb=pb: te.matmul(
                    pb[:, 0:256],
                    WTT[:, 1024 * mc + 128 * ic:1024 * mc + 128 * (ic + 1)],
                    WDWN[:, 256 * mc:256 * (mc + 1)],
                    start=(mc == 0), stop=(mc == 7)),
                    [(WTT, R(1024 * mc + 128 * ic, 1024 * mc + 128 * (ic + 1))),
                     (WDWN, None)], [(pb, R(0, 256))],
                    tag=f"b_de{ic}_{mc}")
            em.op("dve", lambda v, ic=ic, pb=pb: v.tensor_copy(
                WDE2[:, 256 * ic:256 * (ic + 1)], pb[:, 0:256]),
                [(pb, R(0, 256))], [(WDE2, R(256 * ic, 256 * (ic + 1)))],
                tag=f"b_dee{ic}")
        for rc in range(2):
            for tc in range(4):
                pb = PS_A if tc % 2 == 0 else PS_B
                for ic in range(8):
                    em.op("pe", lambda te, rc=rc, tc=tc, ic=ic, pb=pb: te.matmul(
                        pb[:, 0:512],
                        WDE2[:, 256 * ic + 128 * rc:256 * ic + 128 * (rc + 1)],
                        XF[:, 2048 * ic + 512 * tc:2048 * ic + 512 * (tc + 1)],
                        start=(ic == 0), stop=(ic == 7)),
                        [(WDE2, R(256 * ic, 256 * (ic + 1))), (XF, None)],
                        [(pb, R(0, 512))], tag=f"b_lt{rc}_{tc}_{ic}")
                em.op("dve", lambda v, rc=rc, tc=tc, pb=pb: v.tensor_copy(
                    LATT[:, 2048 * rc + 512 * tc:2048 * rc + 512 * (tc + 1)],
                    pb[:, 0:512]),
                    [(pb, R(0, 512))],
                    [(LATT, R(2048 * rc + 512 * tc, 2048 * rc + 512 * (tc + 1)))],
                    tag=f"b_lte{rc}_{tc}")
        hwload(latc.t.ap().rearrange("(rc p) t -> p rc t", p=128),
               LATT.ap.rearrange("p (rc t) -> p rc t", t=2048),
               [(LATT, None), (latc, None)], [(latc, None)], "b_latst")
        em.op("pool", lambda g: g.collective_compute(
            "AllGather", ALU.bypass, replica_groups=PAIRS,
            ins=[latc.t.ap().opt()], outs=[latg.t.ap().opt()]),
            [(latc, None), (latg, None)], [(latg, None)], kind="cc",
            tag="b_cclat")
        swload(WQN.ap.rearrange("p (o d) -> p o d", d=1024),
               wq_b.t.ap().rearrange("(o p) d -> p o d", p=128),
               [(wq_b, None), (WQN, None)], [(WQN, None)], "b_wqld")
        emit_proj_fm("bq", QF, lambda ot, tc: ("q", ot, tc), wbuf=WQN)
        for rc in range(2):
            swload(LATG.ap.rearrange("p (rc st) -> p rc st", rc=2)[:, rc]
                   .rearrange("p (s t) -> p s t", s=2),
                   latg.t.ap().rearrange("(s rc p) t -> p rc s t", p=128, rc=2)
                   [:, rc],
                   [(latg, None), (LATG, None)], [(LATG, None)],
                   f"b_latld{rc}")
        swload(WUPK.ap.rearrange("p (rc d) -> p rc d", d=1024),
               wupk_b.t.ap().rearrange("(rc p) d -> p rc d", p=128),
               [(wupk_b, None), (WUPK, None)], [(WUPK, None)], "b_upkld")
        swload(WUPV.ap.rearrange("p (rc d) -> p rc d", d=1024),
               wupv_b.t.ap().rearrange("(rc p) d -> p rc d", p=128),
               [(wupv_b, None), (WUPV, None)], [(WUPV, None)], "b_upvld")
        for dc in range(8):
            for tc8 in range(8):
                pb = PS_A if tc8 % 2 == 0 else PS_B
                for rc in range(2):
                    em.op("pe", lambda te, dc=dc, tc8=tc8, rc=rc, pb=pb: te.matmul(
                        pb[:, 0:512],
                        WUPK[:, 1024 * rc + 128 * dc:1024 * rc + 128 * (dc + 1)],
                        LATG[:, 4096 * rc + 512 * tc8:4096 * rc + 512 * (tc8 + 1)],
                        start=(rc == 0), stop=(rc == 1)),
                        [(WUPK, None), (LATG, None)], [(pb, R(0, 512))],
                        tag=f"b_kr{dc}_{tc8}_{rc}")
                sl = tc8 % 4
                em.op("dve", lambda v, sl=sl, pb=pb: v.tensor_copy(
                    KST[:, 512 * sl:512 * (sl + 1)], pb[:, 0:512]),
                    [(pb, R(0, 512))], [(KST, R(512 * sl, 512 * (sl + 1)))],
                    tag=f"b_kre{dc}_{tc8}")
                hwload(kb_d.t[128 * dc:128 * (dc + 1), 512 * tc8:512 * (tc8 + 1)],
                       KST[:, 512 * sl:512 * (sl + 1)],
                       [(KST, R(512 * sl, 512 * (sl + 1)))],
                       [(kb_d, ("c", dc, tc8))], f"b_krs{dc}_{tc8}")
        for vt in range(32):
            for n in range(2):
                pb = PS_A if n == 0 else PS_B
                for rc in range(2):
                    em.op("pe", lambda te, vt=vt, n=n, rc=rc, pb=pb: te.matmul(
                        pb[:, 0:512],
                        LATG[:, 4096 * rc + 128 * vt:4096 * rc + 128 * (vt + 1)],
                        WUPV[:, 1024 * rc + 512 * n:1024 * rc + 512 * (n + 1)],
                        start=(rc == 0), stop=(rc == 1)),
                        [(WUPV, None), (LATG, None)], [(pb, R(0, 512))],
                        tag=f"b_vr{vt}_{n}_{rc}")
                em.op("dve", lambda v, n=n, pb=pb: v.tensor_copy(
                    VST[:, 512 * n:512 * (n + 1)], pb[:, 0:512]),
                    [(pb, R(0, 512))], [(VST, R(512 * n, 512 * (n + 1)))],
                    tag=f"b_vre{vt}_{n}")
            hwload(vb_d.t[128 * vt:128 * (vt + 1), :], VST[:, 0:1024],
                   [(VST, R(0, 1024))], [(vb_d, ("t", vt))], f"b_vrs{vt}")
        swload(WON.ap.rearrange("p (o d) -> p o d", d=1024),
               wo_b.t.ap().rearrange("(o p) d -> p o d", p=128),
               [(wo_b, None), (WON, None)], [(WON, None)], "b_wold")
        hwload(DSAB.t[:].bitcast(F32), dsab_in.t.ap(),
               [(dsab_in, None), (DSAB, None)], [(DSAB, None)], "b_dsld")
        swload(DCRB.t[0:1, :], dcr_in.t.ap(),
               [(dcr_in, None), (DCRB, None)], [(DCRB, None)], "b_dcld")
        EAm = DCRB.t[0:1, 0:128]
        EAp = DCRB.t[0:1, 128:256]
        EBp = DCRB.t[0:1, 256:768]
        EBm = DCRB.t[0:1, 768:1280]

        def dcol(base, qt, ci):
            c = base + 8 * qt + ci
            return DSAB.t[:, 2 * c:2 * c + 2].bitcast(F32)
        vb3 = vb_d.t.ap().rearrange("(blk p) d -> p blk d", p=128)
        for grp in range(8):
            qts = [2 * grp, 2 * grp + 1]
            for bk in range(4):
                em.op("pe", lambda te, bk=bk: te.matmul(
                    PS_O[:, 512 * bk:512 * (bk + 1)], ZROW[0:1, 0:128],
                    ZROW[0:1, 0:512], start=True, stop=False,
                    skip_group_check=True),
                    [(ZROW, None)],
                    [(PS_O, ("oc", 128 * (4 * bk))), (PS_O, ("oc", 128 * (4 * bk + 1))),
                     (PS_O, ("oc", 128 * (4 * bk + 2))), (PS_O, ("oc", 128 * (4 * bk + 3)))],
                    tag=f"b_oz{grp}_{bk}")
            for ci in range(8):
                slot = ci % 2
                hwload(
                    KCH.ap.rearrange("p (s o c) -> p s o c", s=2, c=512)[:, slot],
                    kb_d.t.ap().rearrange("(o p) c -> p o c", p=128)
                    [:, :, 512 * ci:512 * (ci + 1)],
                    [(kb_d, None)], [(KCH, ("s", slot))], f"b_kch{grp}_{ci}")
                hwload(
                    VCH.ap.rearrange("p (s u d) -> p s u d", s=2, d=1024)[:, slot],
                    vb3[:, 4 * ci:4 * (ci + 1)],
                    [(vb_d, None)], [(VCH, ("vs", slot))], f"b_vch{grp}_{ci}")
                for qi, qt in enumerate(qts):
                    em.op("pe", lambda te, EAm=EAm, EBp=EBp: te.matmul(
                        PS_DEC[:, 0:512], EAm, EBp, start=True, stop=True),
                        [(DCRB, None)], [(PS_DEC, R(0, 512))],
                        tag=f"b_dA{grp}_{ci}_{qi}")
                    em.op("dve", lambda v, qt=qt, ci=ci: v.tensor_scalar_mul(
                        MSKG[:, 0:512], PS_DEC[:, 0:512], dcol(0, qt, ci)),
                        [(PS_DEC, R(0, 512)), (DSAB, None), (MSKG, None)],
                        [(MSKG, None)], tag=f"b_dsa{grp}_{ci}_{qi}")
                    em.op("pe", lambda te, EAp=EAp, EBm=EBm: te.matmul(
                        PS_DEC[:, 0:512], EAp, EBm, start=True, stop=True),
                        [(DCRB, None)], [(PS_DEC, R(0, 512))],
                        tag=f"b_dB{grp}_{ci}_{qi}")
                    em.op("dve", lambda v, qt=qt, ci=ci: v.scalar_tensor_tensor(
                        MSKG[:, 0:512], PS_DEC[:, 0:512], dcol(128, qt, ci),
                        MSKG[:, 0:512], ALU.mult, ALU.min),
                        [(PS_DEC, R(0, 512)), (DSAB, None), (MSKG, None)],
                        [(MSKG, None)], tag=f"b_dsb{grp}_{ci}_{qi}")
                    def b_front(hh, qt=qt, qi=qi, ci=ci, slot=slot):
                        ot, hi = hh // 2, hh % 2
                        spb = PS_A if (hh + qi) % 2 == 0 else PS_B
                        qsl = QF.t[64 * hi:64 * (hi + 1),
                                   2048 * ot + 128 * qt:2048 * ot + 128 * (qt + 1)]
                        ksl = KCH.t[64 * hi:64 * (hi + 1),
                                    4096 * slot + 512 * ot:4096 * slot + 512 * (ot + 1)]
                        em.op("pe", lambda te, qsl=qsl, ksl=ksl, spb=spb: te.matmul(
                            spb[:, 0:512], qsl, ksl, start=True, stop=True),
                            [(QF, None), (KCH, ("s", slot))],
                            [(spb, R(0, 512))], tag=f"b_qk{grp}_{ci}_{qt}_{hh}")
                        em.op("dve", lambda v, spb=spb: v.tensor_mul(
                            spb[:, 0:512], spb[:, 0:512], MSKG[:, 0:512]),
                            [(spb, R(0, 512)), (MSKG, None)],
                            [(spb, R(0, 512))], tag=f"b_dm{grp}_{ci}_{qt}_{hh}")
                        em.op("dve", lambda v, spb=spb: v.tensor_scalar(
                            spb[:, 0:512], spb[:, 0:512], 10000.0, -10000.0,
                            ALU.min, ALU.max),
                            [(spb, R(0, 512))], [(spb, R(0, 512))],
                            tag=f"b_cl{grp}_{ci}_{qt}_{hh}")
                        pslot = (qi * NH + hh) % 2
                        pdst = PCH[:, 512 * pslot:512 * (pslot + 1)]
                        lcol = 128 * qi + 8 * hh + ci
                        em.op("act", lambda a, spb=spb, pdst=pdst, lcol=lcol:
                              a.activation(
                            pdst, spb[:, 0:512], AF.Exp,
                            accum_out=LPQ[:, lcol:lcol + 1]),
                            [(spb, R(0, 512)), (FC, None)],
                            [(PCH, ("s", pslot)), (LPQ, R(lcol, lcol + 1))],
                            tag=f"b_exp{grp}_{ci}_{qt}_{hh}")

                    def b_back(hh, qt=qt, qi=qi, ci=ci, slot=slot):
                        pslot = (qi * NH + hh) % 2
                        pdst = PCH[:, 512 * pslot:512 * (pslot + 1)]
                        for c2 in range(4):
                            em.op("pe", lambda te, pdst=pdst, c2=c2: te.matmul(
                                PS_T[:, 128 * c2:128 * (c2 + 1)],
                                pdst[:, 128 * c2:128 * (c2 + 1)], IDN.ap,
                                start=True, stop=True),
                                [(PCH, ("s", pslot)), (IDN, None)],
                                [(PS_T, R(128 * c2, 128 * (c2 + 1)))],
                                tag=f"b_T{grp}_{ci}_{qt}_{hh}_{c2}")
                        em.op("dve", lambda v, pslot=pslot: v.tensor_copy(
                            PTC[:, 512 * pslot:512 * (pslot + 1)], PS_T[:, 0:512]),
                            [(PS_T, R(0, 512))], [(PTC, ("s", pslot))],
                            tag=f"b_Te{grp}_{ci}_{qt}_{hh}")
                        pr = hh // 2
                        orow = 64 * (hh % 2)
                        ocol = 1024 * qi + 128 * pr
                        for u in range(4):
                            vsl = VCH.t[:, 4096 * slot + 1024 * u + 64 * hh:
                                        4096 * slot + 1024 * u + 64 * (hh + 1)]
                            ptsl = PTC.t[:, 512 * pslot + 128 * u:
                                         512 * pslot + 128 * (u + 1)]
                            em.op("pe", lambda te, vsl=vsl, ptsl=ptsl,
                                  orow=orow, ocol=ocol,
                                  sp=(ci == 7 and u == 3): te.matmul(
                                PS_O[orow:orow + 64, ocol:ocol + 128],
                                vsl, ptsl, start=False, stop=sp,
                                tile_position=(0, orow), skip_group_check=True),
                                [(VCH, ("vs", slot)), (PTC, ("s", pslot))],
                                [(PS_O, ("oc", ocol))],
                                tag=f"b_pv{grp}_{ci}_{qt}_{hh}_{u}")

                    b_front(0)
                    for hh in range(1, NH):
                        b_front(hh)
                        b_back(hh - 1)
                    b_back(NH - 1)
            for qi, qt in enumerate(qts):
                em.op("dve", lambda v, qi=qi: v.reduce_sum(
                    LRED[:, 16 * qi:16 * (qi + 1)].unsqueeze(2),
                    LPQ.ap.rearrange("p (q h c) -> p q h c", q=2, h=16)
                    [:, qi, :, 0:8], axis=AX.X),
                    [(LPQ, R(128 * qi, 128 * (qi + 1)))],
                    [(LRED, R(16 * qi, 16 * qi + 16))], tag=f"b_lr{grp}_{qi}")
                em.op("dve", lambda v, qi=qi: v.reciprocal(
                    LRED[:, 32 + 16 * qi:32 + 16 * (qi + 1)],
                    LRED[:, 16 * qi:16 * (qi + 1)]),
                    [(LRED, R(16 * qi, 16 * qi + 16))],
                    [(LRED, R(32 + 16 * qi, 48 + 16 * qi))], tag=f"b_rc{grp}_{qi}")
                em.op("pe", lambda te, qi=qi: te.matmul(
                    PS_T[0:16, 0:128],
                    LRED[:, 32 + 16 * qi:32 + 16 * (qi + 1)], IDNF.ap,
                    start=True, stop=True),
                    [(LRED, R(32 + 16 * qi, 48 + 16 * qi)), (IDNF, None)],
                    [(PS_T, R(0, 128))], tag=f"b_rT{grp}_{qi}")
                em.op("dve", lambda v, qi=qi: v.tensor_copy(
                    RROWB[:, 128 * qi:128 * (qi + 1)], PS_T[0:16, 0:128]),
                    [(PS_T, R(0, 128))], [(RROWB, ("q", qi))],
                    tag=f"b_rTe{grp}_{qi}")
                for pr in range(8):
                    em.op("pe", lambda te, qi=qi, pr=pr: te.matmul(
                        PS_T[:, 128:256],
                        SELP[:, 128 * pr:128 * (pr + 1)],
                        RROWB[:, 128 * qi:128 * (qi + 1)],
                        start=True, stop=True),
                        [(SELP, None), (RROWB, ("q", qi))],
                        [(PS_T, R(128, 256))], tag=f"b_rb{grp}_{qi}_{pr}")
                    em.op("dve", lambda v, qi=qi: v.tensor_copy(
                        RB[:, 128 * qi:128 * (qi + 1)], PS_T[:, 128:256]),
                        [(PS_T, R(128, 256))], [(RB, ("q", qi))],
                        tag=f"b_rbe{grp}_{qi}_{pr}")
                    ocol = 1024 * qi + 128 * pr
                    em.op("dve", lambda v, qi=qi, pr=pr, qt=qt, ocol=ocol:
                          v.tensor_mul(
                        AOF[:, 2048 * pr + 128 * qt:2048 * pr + 128 * (qt + 1)],
                        PS_O[:, ocol:ocol + 128], RB[:, 128 * qi:128 * (qi + 1)]),
                        [(PS_O, ("oc", ocol)), (RB, ("q", qi))],
                        [(AOF, ("p", pr, "t", qt))], tag=f"b_on{grp}_{qi}_{pr}")
        for t in range(NT):
            for n in range(2):
                pb = PS_A if n == 0 else PS_B
                for dc in range(8):
                    em.op("pe", lambda te, t=t, n=n, dc=dc, pb=pb: te.matmul(
                        pb[:, 0:512],
                        AOF[:, 2048 * dc + 128 * t:2048 * dc + 128 * (t + 1)],
                        WON[:, 1024 * dc + 512 * n:1024 * dc + 512 * (n + 1)],
                        start=(dc == 0), stop=(dc == 7)),
                        [(AOF, None), (WON, None)],
                        [(pb, R(0, 512))], tag=f"bo_mm{t}_{n}_{dc}")
                em.op("dve", lambda v, t=t, n=n, pb=pb: v.tensor_add(
                    XO[:, 1024 * t + 512 * n:1024 * t + 512 * (n + 1)],
                    XO[:, 1024 * t + 512 * n:1024 * t + 512 * (n + 1)],
                    pb[:, 0:512]),
                    [(pb, R(0, 512)), (XO, ("t", t))], [(XO, ("t", t))],
                    tag=f"bo_res{t}_{n}")

    CWo = {True: 1, False: 0}

    def xs_col(c, t):
        return CW * c + (1 if c < 4 else 0) + 128 * t

    def tblv(tab, t, full=True, half=None):
        if full:
            return AP(TBL.t, 512 * tab + 32 * t,
                      [[4 * 512, 128], [0, 16], [0, 2], [1, 32]])
        return AP(TBL.t, 512 * tab + 32 * t,
                  [[4 * 512, 128], [0, 16], [1, 32]])

    def v4(ap):
        return ap.rearrange("p (h x j) -> p h x j", h=16, x=2)

    def v3(ap):
        return ap.rearrange("p (h j) -> p h j", h=16)

    def emit_ffn_entry():
        hwload(xo_spill.t.ap().rearrange("(t p) d -> p t d", p=128),
               XO.ap.rearrange("p (t d) -> p t d", d=1024),
               [(XO, None)], [(xo_spill, None)], "spill")
        MUc, E2c, RSc, NMRc, SCRc, VARc = 32, 48, 64, 80, 96, 112
        for t in range(NT):
            xt = XO[:, 1024 * t:1024 * (t + 1)]
            em.op("dve", lambda v, xt=xt, t=t: v.reduce_sum(
                STAT[:, t:t + 1], xt, axis=AX.X),
                [(XO, ("t", t))], [(STAT, R(t, t + 1))], tag=f"fe_sum{t}")
            em.op("act", lambda a, xt=xt, t=t: a.activation(
                TMP[:, 2048:3072], xt, AF.Square,
                accum_out=STAT[:, 16 + t:17 + t]),
                [(XO, ("t", t))], [(TMP, R(2048, 3072)), (STAT, R(16 + t, 17 + t))],
                tag=f"fe_sq{t}")
        MU, E2 = STAT[:, MUc:MUc + 16], STAT[:, E2c:E2c + 16]
        RS, NMR = STAT[:, RSc:RSc + 16], STAT[:, NMRc:NMRc + 16]
        SCR, VAR = STAT[:, SCRc:SCRc + 16], STAT[:, VARc:VARc + 16]
        em.op("dve", lambda v: v.tensor_scalar_mul(MU, STAT[:, 0:16], 1.0 / D),
              [(STAT, R(0, 16))], [(STAT, R(MUc, MUc + 16))], tag="fe_mu")
        em.op("dve", lambda v: v.tensor_scalar_mul(E2, STAT[:, 16:32], 1.0 / D),
              [(STAT, R(16, 32))], [(STAT, R(E2c, E2c + 16))], tag="fe_e2")
        em.op("dve", lambda v: v.tensor_mul(SCR, MU, MU),
              [(STAT, R(MUc, MUc + 16))], [(STAT, R(SCRc, SCRc + 16))], tag="fe_musq")
        em.op("dve", lambda v: v.tensor_sub(VAR, E2, SCR),
              [(STAT, R(E2c, E2c + 16)), (STAT, R(SCRc, SCRc + 16))],
              [(STAT, R(VARc, VARc + 16))], tag="fe_var")
        em.op("act", lambda a: a.activation(SCR, VAR, AF.Ln, bias=1e-5),
              [(STAT, R(VARc, VARc + 16))], [(STAT, R(SCRc, SCRc + 16))], tag="fe_ln")
        em.op("act", lambda a: a.activation(RS, SCR, AF.Exp, scale=-0.5),
              [(STAT, R(SCRc, SCRc + 16))], [(STAT, R(RSc, RSc + 16))], tag="fe_rs")
        em.op("dve", lambda v: v.scalar_tensor_tensor(
            NMR, MU, -1.0, RS, ALU.mult, ALU.mult),
            [(STAT, R(MUc, MUc + 16)), (STAT, R(RSc, RSc + 16))],
            [(STAT, R(NMRc, NMRc + 16))], tag="fe_nmr")
        for t in range(NT):
            xt = XO[:, 1024 * t:1024 * (t + 1)]
            em.op("act", lambda a, xt=xt, t=t: a.activation(
                xt, xt, AF.Identity, scale=RS[:, t:t + 1], bias=NMR[:, t:t + 1]),
                [(XO, ("t", t)), (STAT, R(RSc, NMRc + 16)), (xo_spill, None)],
                [(XO, ("t", t))], tag=f"fe_app{t}")

    def emit_ffn_exit(uid):
        qkf = QKT.t[:].bitcast(F32)
        for t in range(NT):
            hwload(qkf,
                   xo_spill.t.ap().rearrange("(t p) d -> p t d", p=128)[:, t],
                   [(xo_spill, None), (QKT, None)], [(QKT, None)], f"fx{uid}_{t}")
            em.op("dve", lambda v, t=t, qkf=qkf: v.tensor_add(
                XO[:, 1024 * t:1024 * (t + 1)], XO[:, 1024 * t:1024 * (t + 1)],
                qkf),
                [(QKT, None), (XO, ("t", t))], [(XO, ("t", t))], tag=f"fxa{uid}_{t}")

    def emit_layer_weights(li, uid):
        swload(WQKV.ap.rearrange("p (dc m) -> p dc m", m=3072),
               f_wqkv.t.ap()[li].rearrange("(dc p) m -> p dc m", p=128),
               [(f_wqkv, None), (WQKV, None)], [(WQKV, None)], f"{uid}_wqkv")
        swload(WOF.ap.rearrange("p (dc m) -> p dc m", m=1024),
               f_wo.t.ap()[li].rearrange("(dc p) m -> p dc m", p=128),
               [(f_wo, None), (WOF, None)], [(WOF, None)], f"{uid}_wo")
        hwload(QS1[0:1, 0:64], f_qs.t.ap()[li].unsqueeze(0),
               [(f_qs, None)], [(QS1, ("q",))], f"{uid}_qs")
        hwload(QS1[0:1, 64:128], f_ks.t.ap()[li].unsqueeze(0),
               [(f_ks, None)], [(QS1, ("k",))], f"{uid}_ks")
        em.op("pe", lambda te: te.matmul(PS_T[:, 0:128], ONEF[0:1, :],
                                         QS1[0:1, 0:128], start=True, stop=True),
              [(ONEF, None), (QS1, ("q",)), (QS1, ("k",))], [(PS_T, R(0, 128))],
              tag=f"{uid}_qsb")
        em.op("dve", lambda v: v.tensor_copy(QSB.ap, PS_T[:, 0:128]),
              [(PS_T, R(0, 128))], [(QSB, None)], tag=f"{uid}_qsbe")
        hwload(GBC[:, 0:8], f_lng.t.ap()[li],
               [(f_lng, None), (GBC, R(0, 8))], [(GBC, R(0, 8))], f"{uid}_g")
        hwload(GBC[:, 8:16], f_lnb.t.ap()[li],
               [(f_lnb, None), (GBC, R(8, 16))], [(GBC, R(8, 16))], f"{uid}_b")
        hwload(B1C.ap, f_b1.t.ap()[li],
               [(f_b1, None), (B1C, None)], [(B1C, None)], f"{uid}_b1")
        hwload(B2C.ap, f_b2.t.ap()[li],
               [(f_b2, None), (B2C, None)], [(B2C, None)], f"{uid}_b2")

    def emit_halo_send(uid):
        hwload(bndx.t.ap(), XO[127:128, 1024 * 15:1024 * 16],
               [(XO, ("t", 15)), (bndx, None)], [(bndx, None)], f"{uid}_bx")
        em.op("pool", lambda g: g.collective_compute(
            "AllGather", ALU.bypass, replica_groups=PAIRS,
            ins=[bndx.t.ap().opt()], outs=[bndxg.t.ap().opt()]),
            [(bndx, None), (bndxg, None)], [(bndxg, None)], kind="cc",
            tag=f"{uid}_cc1")

    def emit_halo_recv(uid):
        swload(HB[0:1, 0:1024], bndxg.t.ap()[0:1, :], [(bndxg, None), (HB, None)],
               [(HB, None)], f"{uid}_hbl")
        em.op("dve", lambda v: v.tensor_scalar_mul(HB[0:1, 0:1024], HB[0:1, 0:1024], HSC[0:1, 0:1]),
              [(HB, None), (HSC, None)], [(HB, None)], tag=f"{uid}_hbm")

    def emit_halo_x(uid):
        emit_halo_send(uid)
        emit_halo_recv(uid)

    def xst_lhs(sl, c):
        if c < 4:
            return 1028 * sl + 128 * c
        return 1028 * sl + 512 + 129 * (c - 4)

    def emit_attn_sub(li, uid):
        emit_layer_weights(li, uid)
        emit_halo_send(uid)
        torder = [14, 15] + list(range(14))
        LLc, RCc, SCQc, RKc = 0, 16, 32, 48

        def att_tile(t):
            if t == 14:
                for tt in (14, 15):
                    rs2 = tt % 4
                    hwload(AP(KRN.t, 128 * rs2, [[6144, 128], [768, 8], [1, 128]]),
                           kst_d.t.ap().rearrange("(c p) s -> p c s", p=128)
                           [:, :, 128 * (tt - 14):128 * (tt - 13)],
                           [(kst_d, None), (KRN, None)],
                           [(KRN, ("r", rs2, 0)), (KRN, ("r", rs2, 1))],
                           f"{uid}_krl{tt}")
                    hwload(AP(VRN.t, 1024 * rs2, [[6144, 128], [1, 1024]]),
                           vst_d.t.ap()[128 * (tt - 14):128 * (tt - 13), :],
                           [(vst_d, None), (VRN, None)],
                           [(VRN, R(1024 * rs2, 1024 * (rs2 + 1)))],
                           f"{uid}_vrl{tt}")
                hwload(QFT[:, 0:2048], qst_d.t.ap(),
                       [(qst_d, None), (QFT, ("s", 0)), (QFT, ("s", 1))],
                       [(QFT, ("s", 0)), (QFT, ("s", 1))], f"{uid}_qrl")
            ob = 1024 * (t % 2)
            for bk in range(2):
                em.op("pe", lambda te, ob=ob, bk=bk: te.matmul(
                    PS_O[:, ob + 512 * bk:ob + 512 * (bk + 1)], ZROW[0:1, 0:128],
                    ZROW[0:1, 0:512], start=True, stop=False,
                    skip_group_check=True),
                    [(ZROW, None)], [(PS_O, ("oz", t % 2, bk))],
                    tag=f"{uid}_oz{t}_{bk}")
            subs = []
            for si, kt in enumerate((t - 2, t - 1, t)):
                slot = (4 + (kt + 2)) if kt < 0 else kt % 4
                if si == 0:
                    mk = LMA if t <= 1 else TRIA
                elif si == 1:
                    mk = LM0B if t == 0 else None
                else:
                    mk = TRIC
                subs.append((si, slot, mk))
            w0 = (t - 2) % 4
            def head_front(hh):
                ot, hi = hh // 2, hh % 2
                spb = PS_A if hh % 2 == 0 else PS_B
                qsl = QFT.t[64 * hi:64 * (hi + 1),
                            1024 * (t % 2) + 128 * ot:1024 * (t % 2) + 128 * (ot + 1)]
                if t == 0:
                    injs = [(0, 0, 384)]
                elif t == 1:
                    injs = [(768, 0, 384)]
                else:
                    injs = [(384, 0, 384)]
                for (mo, so, wdt) in injs:
                    em.op("pe", lambda te, mo=mo, so=so, wdt=wdt, spb=spb:
                          te.matmul(spb[:, so:so + wdt], IDN.ap,
                                    MSKG[:, mo:mo + wdt],
                                    start=(so == injs[0][1]), stop=False),
                          [(IDN, None), (MSKG, None)],
                          [(spb, R(0, 384))],
                          tag=f"{uid}_in{t}_{hh}_{so}")
                em.op("pe", lambda te, qsl=qsl, spb=spb, ot=ot: te.matmul(
                    spb[:, 0:384], qsl,
                    KRN.t[64 * (hh % 2):64 * (hh % 2) + 64,
                          768 * ot + 128 * w0:768 * ot + 128 * w0 + 384],
                    start=False, stop=True),
                    [(QFT, ("s", t % 2)), (KRN, None)],
                    [(spb, R(0, 384))],
                    tag=f"{uid}_qk{t}_{hh}")
                em.op("dve", lambda v, spb=spb, hh=hh: v.reduce_max(
                    LAT[:, 160 + hh:161 + hh], spb[:, 0:384], axis=AX.X,
                    negate=True),
                    [(spb, R(0, 384))], [(LAT, R(160 + hh, 161 + hh))],
                    tag=f"{uid}_mx{t}_{hh}")
                ps = (hh % 2)
                em.op("act", lambda a, spb=spb, hh=hh, ps=ps: a.activation(
                    PCHS[:, 384 * ps:384 * (ps + 1)], spb[:, 0:384], AF.Exp,
                    bias=LAT[:, 160 + hh:161 + hh],
                    accum_out=LAT[:, LLc + hh:LLc + hh + 1]),
                    [(spb, R(0, 384)), (LAT, R(160 + hh, 161 + hh))],
                    [(PCHS, ("s", ps)), (LAT, R(LLc + hh, LLc + hh + 1))],
                    tag=f"{uid}_ex{t}_{hh}")

            def head_back(hh):
                ot, hi = hh // 2, hh % 2
                ps = (hh % 2)
                for si in range(3):
                    em.op("pe", lambda te, ps=ps, si=si: te.matmul(
                        PS_T[:, 128 * si:128 * (si + 1)],
                        PCHS[:, 384 * ps + 128 * si:384 * ps + 128 * (si + 1)],
                        IDN.ap, start=True, stop=True),
                        [(PCHS, ("s", ps)), (IDN, None)],
                        [(PS_T, R(128 * si, 128 * (si + 1)))],
                        tag=f"{uid}_pT{t}_{hh}_{si}")
                em.op("dve", lambda v, ps=ps: v.tensor_copy(
                    PTCS[:, 384 * ps:384 * (ps + 1)], PS_T[:, 0:384]),
                    [(PS_T, R(0, 384))], [(PTCS, ("s", ps))],
                    tag=f"{uid}_pTe{t}_{hh}")
                orow = 64 * hi
                ocol = ob + 128 * ot
                for (si, slot, mk) in subs:
                    vsl = VRN.t[:, 1024 * slot + 64 * hh:1024 * slot + 64 * (hh + 1)]
                    em.op("pe", lambda te, vsl=vsl, ps=ps, si=si, orow=orow,
                          ocol=ocol: te.matmul(
                        PS_O[orow:orow + 64, ocol:ocol + 128], vsl,
                        PTCS[:, 384 * ps + 128 * si:384 * ps + 128 * (si + 1)],
                        start=False, stop=(si == 2), tile_position=(0, orow),
                        skip_group_check=True),
                        [(VRN, None), (PTCS, ("s", ps)),
                         (PS_O, ("oz", t % 2, ot // 2))],
                        [(PS_O, ("oc2", t % 2, ot))],
                        tag=f"{uid}_pv{t}_{hh}_{si}")

            head_front(0)
            for hh in range(1, NH):
                head_front(hh)
                head_back(hh - 1)
            head_back(NH - 1)
            em.op("dve", lambda v: v.reciprocal(
                LAT[:, RCc:RCc + 16], LAT[:, LLc:LLc + 16]),
                [(LAT, R(LLc, LLc + 16))], [(LAT, R(RCc, RCc + 16))],
                tag=f"{uid}_orc{t}")
            em.op("pe", lambda te: te.matmul(
                PS_T[0:16, 0:128], LAT[:, RCc:RCc + 16], IDNF.ap,
                start=True, stop=True),
                [(LAT, R(RCc, RCc + 16)), (IDNF, None)],
                [(PS_T, R(0, 128))], tag=f"{uid}_orT{t}")
            em.op("dve", lambda v: v.tensor_copy(
                RROWB[:, 0:128], PS_T[0:16, 0:128]),
                [(PS_T, R(0, 128))], [(RROWB, ("q", 0))], tag=f"{uid}_orTe{t}")
            for pr in range(8):
                em.op("pe", lambda te, pr=pr: te.matmul(
                    PS_T[:, 128:256], SELP[:, 128 * pr:128 * (pr + 1)],
                    RROWB[:, 0:128], start=True, stop=True),
                    [(SELP, None), (RROWB, ("q", 0))],
                    [(PS_T, R(128, 256))], tag=f"{uid}_orb{t}_{pr}")
                em.op("dve", lambda v: v.tensor_copy(
                    RB[:, 0:128], PS_T[:, 128:256]),
                    [(PS_T, R(128, 256))], [(RB, ("q", 0))],
                    tag=f"{uid}_orbe{t}_{pr}")
                em.op("dve", lambda v, ob=ob, pr=pr: v.tensor_mul(
                    OE[:, 128 * pr:128 * (pr + 1)],
                    PS_O[:, ob + 128 * pr:ob + 128 * (pr + 1)], RB[:, 0:128]),
                    [(PS_O, ("oc2", t % 2, pr)),
                     (PS_O, ("oz", t % 2, pr // 4)), (RB, ("q", 0))],
                    [(OE, ("pr", pr))], tag=f"{uid}_on{t}_{pr}")
            for n in range(2):
                pb = PS_A if n == 0 else PS_B
                for dc in range(8):
                    em.op("pe", lambda te, n=n, dc=dc, pb=pb: te.matmul(
                        pb[:, 0:512], OE[:, 128 * dc:128 * (dc + 1)],
                        WOF[:, 1024 * dc + 512 * n:1024 * dc + 512 * (n + 1)],
                        start=(dc == 0), stop=(dc == 7)),
                        [(OE, None), (WOF, None)], [(pb, R(0, 512))],
                        tag=f"{uid}_wo{t}_{n}_{dc}")
                em.op("dve", lambda v, t=t, n=n, pb=pb: v.tensor_add(
                    XO[:, 1024 * t + 512 * n:1024 * t + 512 * (n + 1)],
                    XO[:, 1024 * t + 512 * n:1024 * t + 512 * (n + 1)],
                    pb[:, 0:512]),
                    [(pb, R(0, 512)), (XO, ("t", t))], [(XO, ("t", t))],
                    tag=f"{uid}_res{t}_{n}")

        for ti, t in enumerate(torder):
            rs = t % 4
            sl = ti % 2
            tmp = TMP[:, 1024 * (ti % 2):1024 * (ti % 2) + 1024]
            em.op("dve", lambda v, t=t, tmp=tmp: v.tensor_copy(
                tmp, XO[:, 1024 * t:1024 * (t + 1)]),
                [(XO, ("t", t))], [(TMP, ("s", ti % 2))], tag=f"{uid}_cast{t}")
            for half in range(2):
                for cc in range(4):
                    c = 4 * half + cc
                    em.op("pe", lambda te, tmp=tmp, c=c, cc=cc: te.matmul(
                        PS_T[:, 128 * cc:128 * (cc + 1)],
                        tmp[:, 128 * c:128 * (c + 1)], IDN.ap,
                        start=True, stop=True),
                        [(TMP, ("s", ti % 2)), (IDN, None)],
                        [(PS_T, R(128 * cc, 128 * (cc + 1)))],
                        tag=f"{uid}_xT{t}_{c}")
                if half == 0:
                    dst = AP(XST.t, 1028 * sl, [[2056, 128], [128, 4], [1, 128]])
                else:
                    dst = AP(XST.t, 1028 * sl + 512 + 1,
                             [[2056, 128], [129, 4], [1, 128]])
                em.op("dve", lambda v, dst=dst: v.tensor_copy(
                    dst, PS_T[:, 0:512].rearrange("p (c f) -> p c f", f=128)),
                    [(PS_T, R(0, 512))], [(XST, ("b", sl, half))],
                    tag=f"{uid}_xTe{t}_{half}")
            if t == 0:
                emit_halo_recv(uid)
                for c in range(4, 8):
                    em.op("pe", lambda te, c=c: te.matmul(
                        PS_T[:, 500 + c - 4:501 + c - 4],
                        HB[0:1, 128 * c:128 * (c + 1)],
                        IDN[0:1, 0:1], start=True, stop=True),
                        [(HB, None), (IDN, None)],
                        [(PS_T, R(500 + c - 4, 501 + c - 4))],
                        tag=f"{uid}_hT{c}")
                em.op("dve", lambda v, sl=sl: v.tensor_copy(
                    AP(XST.t, 1028 * sl + 512, [[2056, 128], [129, 4], [1, 1]]),
                    PS_T[:, 500:504].unsqueeze(2)),
                    [(PS_T, R(500, 504))], [(XST, ("bc", sl))],
                    tag=f"{uid}_hTe{t}")
            elif t == 14:
                swload(HB.ap, XO[127:128, 1024 * 13:1024 * 14],
                       [(XO, ("t", 13)), (HB, None)], [(HB, None)],
                       f"{uid}_row14")
                for c in range(4, 8):
                    em.op("pe", lambda te, c=c: te.matmul(
                        PS_T[:, 500 + c - 4:501 + c - 4],
                        HB[0:1, 128 * c:128 * (c + 1)],
                        IDN[0:1, 0:1], start=True, stop=True),
                        [(HB, None), (IDN, None)],
                        [(PS_T, R(500 + c - 4, 501 + c - 4))],
                        tag=f"{uid}_hT14_{c}")
                em.op("dve", lambda v, sl=sl: v.tensor_copy(
                    AP(XST.t, 1028 * sl + 512, [[2056, 128], [129, 4], [1, 1]]),
                    PS_T[:, 500:504].unsqueeze(2)),
                    [(PS_T, R(500, 504))], [(XST, ("bc", sl))],
                    tag=f"{uid}_hTe{t}")
            else:
                psl = (ti - 1) % 2
                em.op("dve", lambda v, sl=sl, psl=psl: v.tensor_copy(
                    AP(XST.t, 1028 * sl + 512, [[2056, 128], [129, 4], [1, 1]]),
                    AP(XST.t, 1028 * psl + 512 + 128, [[2056, 128], [129, 4], [1, 1]])),
                    [(XST, ("b", psl, 1))], [(XST, ("bc", sl))],
                    tag=f"{uid}_bc{t}")
            for n in range(6):
                pb = PS_A if n % 2 == 0 else PS_B
                for dc in range(8):
                    em.op("pe", lambda te, sl=sl, n=n, dc=dc, pb=pb: te.matmul(
                        pb[:, 0:512],
                        AP(XST.t, xst_lhs(sl, dc), [[2056, 128], [1, 128]]),
                        WQKV[:, 3072 * dc + 512 * n:3072 * dc + 512 * (n + 1)],
                        start=(dc == 0), stop=(dc == 7)),
                        [(XST, ("b", sl, dc // 4)), (XST, ("bc", sl)),
                         (WQKV, None)],
                        [(pb, R(0, 512))], tag=f"{uid}_qkv{t}_{n}_{dc}")
                if n < 4:
                    dst = QKT[:, 512 * n:512 * (n + 1)]
                    em.op("act", lambda a, pb=pb, dst=dst: a.activation(
                        dst, pb[:, 0:512], AF.Identity),
                        [(pb, R(0, 512))], [(QKT, ("n", n))],
                        tag=f"{uid}_qke{t}_{n}")
                    em.op("act", lambda a, pb=pb: a.activation(
                        TMP[:, 2048:2560], pb[:, 0:512], AF.Square),
                        [(pb, R(0, 512))], [(TMP, R(2048, 2560))],
                        tag=f"{uid}_sq{t}_{n}")
                    lc = 64 + 8 * n
                    em.op("dve", lambda v, lc=lc: v.reduce_sum(
                        LAT[:, lc:lc + 8].unsqueeze(2),
                        TMP[:, 2048:2560].rearrange("p (h j) -> p h j", h=8),
                        axis=AX.X),
                        [(TMP, R(2048, 2560))], [(LAT, R(lc, lc + 8))],
                        tag=f"{uid}_ssq{t}_{n}")
                else:
                    em.op("act", lambda a, pb=pb, n=n, rs=rs: a.activation(
                        VRN[:, 1024 * rs + 512 * (n - 4):1024 * rs + 512 * (n - 3)],
                        pb[:, 0:512], AF.Identity),
                        [(pb, R(0, 512))],
                        [(VRN, R(1024 * rs + 512 * (n - 4), 1024 * rs + 512 * (n - 3)))],
                        tag=f"{uid}_ve{t}_{n}")
            em.op("act", lambda a: a.activation(LAT[:, 96:112], LAT[:, 64:80], AF.Ln),
                  [(LAT, R(64, 80))], [(LAT, R(96, 112))], tag=f"{uid}_lnq{t}")
            em.op("act", lambda a: a.activation(LAT[:, 96:112], LAT[:, 96:112],
                                                AF.Exp, scale=0.5,
                                                bias=-2.0794415416798357),
                  [(LAT, R(96, 112))], [(LAT, R(96, 112))], tag=f"{uid}_exq{t}")
            em.op("dve", lambda v: v.tensor_scalar_max(LAT[:, 96:112],
                                                       LAT[:, 96:112], 1.25e-13),
                  [(LAT, R(96, 112))], [(LAT, R(96, 112))], tag=f"{uid}_mxq{t}")
            em.op("dve", lambda v: v.reciprocal(LAT[:, SCQc:SCQc + 16],
                                                LAT[:, 96:112]),
                  [(LAT, R(96, 112))], [(LAT, R(SCQc, SCQc + 16))],
                  tag=f"{uid}_rcq{t}")
            em.op("act", lambda a: a.activation(LAT[:, 96:112], LAT[:, 80:96], AF.Ln),
                  [(LAT, R(80, 96))], [(LAT, R(96, 112))], tag=f"{uid}_lnk{t}")
            em.op("act", lambda a: a.activation(LAT[:, 96:112], LAT[:, 96:112],
                                                AF.Exp, scale=0.5),
                  [(LAT, R(96, 112))], [(LAT, R(96, 112))], tag=f"{uid}_exk{t}")
            em.op("dve", lambda v: v.tensor_scalar_max(LAT[:, 96:112],
                                                       LAT[:, 96:112], 1e-12),
                  [(LAT, R(96, 112))], [(LAT, R(96, 112))], tag=f"{uid}_mxk{t}")
            em.op("dve", lambda v: v.reciprocal(LAT[:, RKc:RKc + 16],
                                                LAT[:, 96:112]),
                  [(LAT, R(96, 112))], [(LAT, R(RKc, RKc + 16))],
                  tag=f"{uid}_rck{t}")
            for (src0, dstb, tabc, tabs, qk) in (
                    (0, QR, 0, 1, "q"), (1024, KQR, 2, 3, "k")):
                em.op("dve", lambda v, src0=src0, qk=qk: v.tensor_tensor(
                    TMP[:, 2048:3072].rearrange("p (h j) -> p h j", h=16),
                    QKT[:, src0:src0 + 1024].rearrange("p (h j) -> p h j", h=16),
                    AP(QSB.t, 0 if qk == "q" else 64, [[128, 128], [0, 16], [1, 64]]),
                    op=ALU.mult),
                    [(QKT, ("n", 0)), (QKT, ("n", 1)), (QKT, ("n", 2)),
                     (QKT, ("n", 3)), (QSB, None)],
                    [(TMP, R(2048, 3072))], tag=f"{uid}_qs{t}_{qk}")
                em.op("dve", lambda v, tabc=tabc, t=t: v.tensor_tensor(
                    v4(TMP[:, 3072:4096]), v4(TMP[:, 2048:3072]), tblv(tabc, t),
                    op=ALU.mult),
                    [(TMP, R(2048, 3072)), (TBL, None)], [(TMP, R(3072, 4096))],
                    tag=f"{uid}_t1{t}_{qk}")
                em.op("dve", lambda v, tabs=tabs, t=t, dstb=dstb: v.tensor_tensor(
                    AP(dstb.t, 0, [[1024, 128], [64, 16], [1, 32]]),
                    AP(TMP.t, 2048 + 32, [[4096, 128], [64, 16], [1, 32]]),
                    tblv(tabs, t, full=False), op=ALU.mult),
                    [(TMP, R(2048, 3072)), (TBL, None)], [(dstb, R(0, 1024))],
                    tag=f"{uid}_e1{t}_{qk}")
                em.op("dve", lambda v, tabs=tabs, t=t, dstb=dstb: v.tensor_tensor(
                    AP(dstb.t, 32, [[1024, 128], [64, 16], [1, 32]]),
                    AP(TMP.t, 2048 + 0, [[4096, 128], [64, 16], [1, 32]]),
                    tblv(tabs, t, full=False), op=ALU.mult),
                    [(TMP, R(2048, 3072)), (TBL, None)], [(dstb, R(0, 1024))],
                    tag=f"{uid}_e2{t}_{qk}")
                em.op("dve", lambda v, dstb=dstb: v.tensor_sub(
                    AP(dstb.t, 0, [[1024, 128], [64, 16], [1, 32]]),
                    AP(TMP.t, 3072 + 0, [[4096, 128], [64, 16], [1, 32]]),
                    AP(dstb.t, 0, [[1024, 128], [64, 16], [1, 32]])),
                    [(TMP, R(3072, 4096)), (dstb, R(0, 1024))],
                    [(dstb, R(0, 1024))], tag=f"{uid}_c1{t}_{qk}")
                em.op("dve", lambda v, dstb=dstb: v.tensor_add(
                    AP(dstb.t, 32, [[1024, 128], [64, 16], [1, 32]]),
                    AP(TMP.t, 3072 + 32, [[4096, 128], [64, 16], [1, 32]]),
                    AP(dstb.t, 32, [[1024, 128], [64, 16], [1, 32]])),
                    [(TMP, R(3072, 4096)), (dstb, R(0, 1024))],
                    [(dstb, R(0, 1024))], tag=f"{uid}_c2{t}_{qk}")
                scol = SCQc if qk == "q" else RKc
                em.op("dve", lambda v, dstb=dstb, scol=scol: v.tensor_tensor(
                    dstb.ap.rearrange("p (h j) -> p h j", h=16),
                    dstb.ap.rearrange("p (h j) -> p h j", h=16),
                    AP(LAT.t, scol, [[256, 128], [1, 16], [0, 64]]),
                    op=ALU.mult),
                    [(dstb, R(0, 1024)), (LAT, R(scol, scol + 16))],
                    [(dstb, R(0, 1024))], tag=f"{uid}_sc{t}_{qk}")
            for (srcb, qk) in ((QR, "q"), (KQR, "k")):
                for half in range(2):
                    for cc in range(4):
                        c = 4 * half + cc
                        em.op("pe", lambda te, srcb=srcb, c=c, cc=cc: te.matmul(
                            PS_T[:, 128 * cc:128 * (cc + 1)],
                            srcb[:, 128 * c:128 * (c + 1)], IDN.ap,
                            start=True, stop=True),
                            [(srcb, R(0, 1024)), (IDN, None)],
                            [(PS_T, R(128 * cc, 128 * (cc + 1)))],
                            tag=f"{uid}_fT{t}_{qk}{c}")
                    if qk == "q":
                        dst = AP(QFT.t, 1024 * (t % 2) + 128 * 4 * half,
                                 [[2048, 128], [128, 4], [1, 128]])
                        wtag = [(QFT, ("s", t % 2))]
                    else:
                        dst = AP(KRN.t, 768 * 4 * half + 128 * rs,
                                 [[6144, 128], [768, 4], [1, 128]])
                        wtag = [(KRN, ("r", rs, half))]
                    em.op("dve", lambda v, dst=dst: v.tensor_copy(
                        dst, PS_T[:, 0:512].rearrange("p (c f) -> p c f", f=128)),
                        [(PS_T, R(0, 512))], wtag, tag=f"{uid}_fTe{t}_{qk}{half}")
                    if qk == "k" and rs < 2:
                        em.op("dve", lambda v, half=half, rs=rs: v.tensor_copy(
                            AP(KRN.t, 768 * 4 * half + 128 * (4 + rs),
                               [[6144, 128], [768, 4], [1, 128]]),
                            AP(KRN.t, 768 * 4 * half + 128 * rs,
                               [[6144, 128], [768, 4], [1, 128]])),
                            [(KRN, ("r", rs, half))],
                            [(KRN, ("rw", rs, half))],
                            tag=f"{uid}_fTw{t}_{qk}{half}")
            if t == 15:
                for (dstd, tg) in ((kb, "kb"), (kst_d, "kst")):
                    hwload(dstd.t.ap().rearrange("(c p) s -> p c s", p=128),
                           AP(KRN.t, 256, [[6144, 128], [768, 8], [1, 256]]),
                           [(KRN, None)], [(dstd, None)], f"{uid}_{tg}")
                for (dstd, tg) in ((vb, "vb"), (vst_d, "vst")):
                    hwload(dstd.t.ap().rearrange("(u p) d -> p u d", p=128),
                           AP(VRN.t, 2048, [[6144, 128], [1024, 2], [1, 1024]]),
                           [(VRN, None)], [(dstd, None)], f"{uid}_{tg}")
                em.op("pool", lambda g: g.collective_compute(
                    "AllGather", ALU.bypass, replica_groups=PAIRS,
                    ins=[kb.t.ap().opt()], outs=[kbg.t.ap().opt()]),
                    [(kb, None), (kbg, None)], [(kbg, None)], kind="cc",
                    tag=f"{uid}_cck")
                em.op("pool", lambda g: g.collective_compute(
                    "AllGather", ALU.bypass, replica_groups=PAIRS,
                    ins=[vb.t.ap().opt()], outs=[vbg.t.ap().opt()]),
                    [(vb, None), (vbg, None)], [(vbg, None)], kind="cc",
                    tag=f"{uid}_ccv")
                hwload(AP(KRN.t, 256, [[6144, 128], [768, 8], [1, 256]]),
                       kbg.t.ap()[0:1024].rearrange("(c p) s -> p c s", p=128),
                       [(kbg, None), (KRN, None)],
                       [(KRN, ("r", 2, 0)), (KRN, ("r", 2, 1)),
                        (KRN, ("r", 3, 0)), (KRN, ("r", 3, 1))], f"{uid}_khl")
                hwload(AP(VRN.t, 4096, [[6144, 128], [1024, 2], [1, 1024]]),
                       vbg.t.ap()[0:256].rearrange("(u p) d -> p u d", p=128),
                       [(vbg, None), (VRN, None)],
                       [(VRN, R(4096, 6144))], f"{uid}_vhl")
                hwload(qst_d.t.ap(), QFT[:, 0:2048],
                       [(QFT, ("s", 0)), (QFT, ("s", 1)), (qst_d, None)],
                       [(qst_d, None)], f"{uid}_qst")
            if ti >= 3:
                att_tile(ti - 3)
        for t2 in (13, 14, 15):
            att_tile(t2)

    def xnf_col(c):
        return 1024 * c if c < 4 else 4096 + 1025 * (c - 4)

    def emit_ffn_sub(li, uid):
        fsect = int(os.environ.get("KFSECT", "9"))
        emit_halo_x(uid)
        em.op("dve", lambda v: v.reduce_sum(
            HS[0:1, 0:2].unsqueeze(2),
            HB[0:1, 0:1024].rearrange("p (x j) -> p x j", x=2), axis=AX.X),
            [(HB, None)], [(HS, R(0, 2))], tag=f"{uid}_hs")
        em.op("act", lambda a: a.activation(
            TMP[0:1, 2048:2560], HB[0:1, 512:1024], AF.Square,
            accum_out=HS[0:1, 3:4]),
            [(HB, None)], [(TMP, R(2048, 2560)), (HS, R(3, 4))], tag=f"{uid}_hq")
        for t in range(NT):
            em.op("dve", lambda v, t=t: v.reduce_sum(
                AP(STAT.t, t, [[128, 128], [32, 2], [1, 1]]),
                XO[:, 1024 * t:1024 * (t + 1)].rearrange("p (x j) -> p x j", x=2),
                axis=AX.X),
                [(XO, ("t", t))], [(STAT, R(t, t + 1)), (STAT, R(32 + t, 33 + t))],
                tag=f"{uid}_s{t}")
            em.op("act", lambda a, t=t: a.activation(
                TMP[:, 2048:2560], XO[:, 1024 * t:1024 * t + 512], AF.Square,
                accum_out=STAT[:, 16 + t:17 + t]),
                [(XO, ("t", t))], [(TMP, R(2048, 2560)), (STAT, R(16 + t, 17 + t))],
                tag=f"{uid}_qa{t}")
            em.op("act", lambda a, t=t: a.activation(
                TMP[:, 2048:2560], XO[:, 1024 * t + 512:1024 * (t + 1)], AF.Square,
                accum_out=STAT[:, 48 + t:49 + t]),
                [(XO, ("t", t))], [(TMP, R(2048, 2560)), (STAT, R(48 + t, 49 + t))],
                tag=f"{uid}_qb{t}")
        em.op("pe", lambda te: te.matmul(PS_T[:, 0:32], SHUF.ap, STAT[:, 32:64],
                                         start=True, stop=True),
              [(SHUF, None), (STAT, R(32, 64))], [(PS_T, R(0, 32))],
              tag=f"{uid}_shu")
        em.op("pe", lambda te: te.matmul(PS_T[0:1, 1:16], E127.ap, STAT[:, 32:47],
                                         start=True, stop=True,
                                         skip_group_check=True),
              [(E127, None), (STAT, R(32, 47))], [(PS_T, R(0, 32))],
              tag=f"{uid}_e1")
        em.op("pe", lambda te: te.matmul(PS_T[0:1, 17:32], E127.ap, STAT[:, 48:63],
                                         start=True, stop=True,
                                         skip_group_check=True),
              [(E127, None), (STAT, R(48, 63))], [(PS_T, R(0, 32))],
              tag=f"{uid}_e2")
        em.op("pe", lambda te: te.matmul(PS_T[0:1, 0:1], HS[0:1, 1:2],
                                         ONEF[0:1, 0:1], start=True, stop=True,
                                         skip_group_check=True),
              [(HS, R(0, 2)), (ONEF, None)], [(PS_T, R(0, 32))], tag=f"{uid}_h1")
        em.op("pe", lambda te: te.matmul(PS_T[0:1, 16:17], HS[0:1, 3:4],
                                         ONEF[0:1, 0:1], start=True, stop=True,
                                         skip_group_check=True),
              [(HS, R(3, 4)), (ONEF, None)], [(PS_T, R(0, 32))], tag=f"{uid}_h2")
        em.op("dve", lambda v: v.tensor_copy(LAT[:, 96:128], PS_T[:, 0:32]),
              [(PS_T, R(0, 32))], [(LAT, R(96, 128))], tag=f"{uid}_she")
        MUc, E2c, VARc, RSc, NMRc, RAc, SCc = 128, 144, 160, 176, 192, 208, 240
        em.op("dve", lambda v: v.tensor_add(LAT[:, MUc:MUc + 16], STAT[:, 0:16],
                                            LAT[:, 96:112]),
              [(STAT, R(0, 16)), (LAT, R(96, 112))], [(LAT, R(MUc, MUc + 16))],
              tag=f"{uid}_sx")
        em.op("dve", lambda v: v.tensor_add(LAT[:, E2c:E2c + 16], STAT[:, 16:32],
                                            LAT[:, 112:128]),
              [(STAT, R(16, 32)), (LAT, R(112, 128))], [(LAT, R(E2c, E2c + 16))],
              tag=f"{uid}_sxx")
        em.op("dve", lambda v: v.tensor_scalar_mul(LAT[:, MUc:MUc + 16],
                                                   LAT[:, MUc:MUc + 16], 1.0 / D),
              [(LAT, R(MUc, MUc + 16))], [(LAT, R(MUc, MUc + 16))], tag=f"{uid}_mu")
        em.op("dve", lambda v: v.tensor_scalar_mul(LAT[:, E2c:E2c + 16],
                                                   LAT[:, E2c:E2c + 16], 1.0 / D),
              [(LAT, R(E2c, E2c + 16))], [(LAT, R(E2c, E2c + 16))], tag=f"{uid}_e2s")
        em.op("dve", lambda v: v.tensor_mul(LAT[:, SCc:SCc + 16],
                                            LAT[:, MUc:MUc + 16],
                                            LAT[:, MUc:MUc + 16]),
              [(LAT, R(MUc, MUc + 16))], [(LAT, R(SCc, SCc + 16))],
              tag=f"{uid}_musq")
        em.op("dve", lambda v: v.tensor_sub(LAT[:, VARc:VARc + 16],
                                            LAT[:, E2c:E2c + 16],
                                            LAT[:, SCc:SCc + 16]),
              [(LAT, R(E2c, E2c + 16)), (LAT, R(SCc, SCc + 16))],
              [(LAT, R(VARc, VARc + 16))], tag=f"{uid}_var")
        em.op("act", lambda a: a.activation(LAT[:, SCc:SCc + 16],
                                            LAT[:, VARc:VARc + 16], AF.Ln,
                                            bias=1e-5),
              [(LAT, R(VARc, VARc + 16))], [(LAT, R(SCc, SCc + 16))],
              tag=f"{uid}_lnv")
        em.op("act", lambda a: a.activation(LAT[:, RSc:RSc + 16],
                                            LAT[:, SCc:SCc + 16], AF.Exp,
                                            scale=-0.5),
              [(LAT, R(SCc, SCc + 16))], [(LAT, R(RSc, RSc + 16))],
              tag=f"{uid}_rstd")
        em.op("dve", lambda v: v.scalar_tensor_tensor(
            LAT[:, NMRc:NMRc + 16], LAT[:, MUc:MUc + 16], -1.0,
            LAT[:, RSc:RSc + 16], ALU.mult, ALU.mult),
            [(LAT, R(MUc, MUc + 16)), (LAT, R(RSc, RSc + 16))],
            [(LAT, R(NMRc, NMRc + 16))], tag=f"{uid}_nmr")
        em.op("pe", lambda te: te.matmul(PS_T[:, 32:64], SHDF.ap,
                                         LAT[:, RSc:RSc + 32], start=True,
                                         stop=True, skip_group_check=True),
              [(SHDF, None), (LAT, R(RSc, RSc + 32)), (PS_T, R(0, 32))],
              [(PS_T, R(32, 64))], tag=f"{uid}_shd")
        em.op("pe", lambda te: te.matmul(PS_T[64:128, 64:79], SEL64.ap,
                                         LAT[:, RSc + 1:RSc + 16], start=True,
                                         stop=True, skip_group_check=True),
              [(SEL64, None), (LAT, R(RSc, RSc + 16))], [(PS_T, R(64, 96))],
              tag=f"{uid}_sl1")
        em.op("pe", lambda te: te.matmul(PS_T[64:128, 80:95], SEL64.ap,
                                         LAT[:, NMRc + 1:NMRc + 16], start=True,
                                         stop=True, skip_group_check=True),
              [(SEL64, None), (LAT, R(NMRc, NMRc + 16))], [(PS_T, R(64, 96))],
              tag=f"{uid}_sl2")
        em.op("dve", lambda v: v.tensor_copy(LAT[:, RAc:RAc + 32], PS_T[:, 32:64]),
              [(PS_T, R(32, 64))], [(LAT, R(RAc, RAc + 32))], tag=f"{uid}_adv")
        em.op("dve", lambda v: v.scalar_tensor_tensor(
            LAT[64:128, RAc:RAc + 15], PS_T[64:128, 64:79], 0.0,
            LAT[64:128, RAc:RAc + 15], ALU.add, ALU.add),
            [(PS_T, R(64, 96)), (LAT, R(RAc, RAc + 32))],
            [(LAT, R(RAc, RAc + 32))], tag=f"{uid}_adv1")
        em.op("dve", lambda v: v.scalar_tensor_tensor(
            LAT[64:128, RAc + 16:RAc + 31], PS_T[64:128, 80:95], 0.0,
            LAT[64:128, RAc + 16:RAc + 31], ALU.add, ALU.add),
            [(PS_T, R(64, 96)), (LAT, R(RAc, RAc + 32))],
            [(LAT, R(RAc, RAc + 32))], tag=f"{uid}_adv2")
        if fsect < 2:
            return
        fsb = int(os.environ.get("KFSB", "7"))
        em.op("dve", lambda v: v.tensor_scalar(
            HBB[0:1, 0:512], HB[0:1, 512:1024],
            LAT[0:1, RSc:RSc + 1], LAT[0:1, NMRc:NMRc + 1],
            ALU.mult, ALU.add),
            [(HB, None), (LAT, R(RSc, RSc + 1)), (LAT, R(NMRc, NMRc + 1))],
            [(HBB, None)], tag=f"{uid}_hba")
        for g2 in range(2):
            if not (fsb & 1):
                pass
            elif g2 == 0:
                for c in range(4):
                    em.op("pe", lambda te, c=c: te.matmul(
                        PS_T[:, 384 + c:385 + c], HBB[0:1, 128 * c:128 * (c + 1)],
                        IDN[0:1, 0:1], start=True, stop=True),
                        [(HBB, None), (IDN, None)], [(PS_T, R(384 + c, 385 + c))],
                        tag=f"{uid}_hT{c}")
                for c in range(4):
                    em.op("act", lambda a, c=c: a.activation(
                        XCR[:, 4 + c:5 + c],
                        PS_T[:, 384 + c:385 + c], AF.Identity,
                        scale=GBC[:, 4 + c:5 + c], bias=GBC[:, 12 + c:13 + c]),
                        [(PS_T, R(384 + c, 385 + c)), (GBC, None)],
                        [(XCR, ("h",))], tag=f"{uid}_hTg{c}")
                em.op("dve", lambda v: v.tensor_copy(
                    AP(XNFG.t, 4096, [[8196, 128], [1025, 4], [1, 1]]),
                    XCR[:, 4:8].unsqueeze(2)),
                    [(XCR, ("h",))], [(XNFG, ("cin",))],
                    tag=f"{uid}_hTe")
            else:
                em.op("dve", lambda v: v.tensor_copy(
                    XCR[:, 0:4].unsqueeze(2),
                    AP(XNFG.t, 4096 + 1024, [[8196, 128], [1025, 4], [1, 1]])),
                    [(XNFG, None)], [(XCR, None)], tag=f"{uid}_cro")
                em.op("dve", lambda v: v.tensor_copy(
                    AP(XNFG.t, 4096, [[8196, 128], [1025, 4], [1, 1]]),
                    XCR[:, 0:4].unsqueeze(2)),
                    [(XCR, None)], [(XNFG, ("cin",))], tag=f"{uid}_cri")
            for tg in range(2):
                if not (fsb & 2):
                    continue
                for tt in range(4):
                    ttl = 4 * tg + tt
                    t = 8 * g2 + ttl
                    tmp = TMP[:, 1024 * tt:1024 * (tt + 1)]
                    em.op("act", lambda a, t=t, tmp=tmp: a.activation(
                        tmp[:, 0:512], XO[:, 1024 * t:1024 * t + 512], AF.Identity,
                        scale=LAT[:, RSc + t:RSc + t + 1],
                        bias=LAT[:, NMRc + t:NMRc + t + 1]),
                        [(XO, ("t", t)), (LAT, R(RSc, RSc + 16)),
                         (LAT, R(NMRc, NMRc + 16))],
                        [(TMP, ("f", tt, 0))], tag=f"{uid}_apA{t}")
                    em.op("act", lambda a, t=t, tmp=tmp: a.activation(
                        tmp[:, 512:1024], XO[:, 1024 * t + 512:1024 * (t + 1)],
                        AF.Identity, scale=LAT[:, RAc + t:RAc + t + 1],
                        bias=LAT[:, RAc + 16 + t:RAc + 17 + t]),
                        [(XO, ("t", t)), (LAT, R(RAc, RAc + 32))],
                        [(TMP, ("f", tt, 1))], tag=f"{uid}_apB{t}")
                for c in range(8):
                    if not (fsb & 4):
                        continue
                    for tt in range(4):
                        tmp = TMP[:, 1024 * tt:1024 * (tt + 1)]
                        em.op("pe", lambda te, tmp=tmp, c=c, tt=tt: te.matmul(
                            PS_T[:, 128 * tt:128 * (tt + 1)],
                            tmp[:, 128 * c:128 * (c + 1)], IDN.ap,
                            start=True, stop=True),
                            [(TMP, ("f", tt, c // 4)), (IDN, None)],
                            [(PS_T, R(128 * tt, 128 * (tt + 1)))],
                            tag=f"{uid}_nT{g2}_{tg}_{c}_{tt}")
                    base = xnf_col(c) + (0 if c < 4 else 1) + 512 * tg
                    em.op("act", lambda a, c=c, base=base: a.activation(
                        AP(XNFG.t, base, [[8196, 128], [1, 512]]),
                        PS_T[:, 0:512], AF.Identity, scale=GBC[:, c:c + 1],
                        bias=GBC[:, 8 + c:9 + c]),
                        [(PS_T, R(0, 512)), (GBC, None)],
                        [(XNFG, ("nf", tg, c))], tag=f"{uid}_nTe{g2}_{tg}_{c}")
            if fsect < 3:
                continue
            for ht in range(22):
                hs2 = 128 * ht
                hw = min(128, 2730 - hs2)
                sl = ht % 2
                swload(AP(W1T.t, 2048 * sl, [[4096, 128], [128, 8], [1, hw]]),
                       f_w1.t.ap()[li, :, hs2:hs2 + hw]
                       .rearrange("(dc p) m -> p dc m", p=128),
                       [(f_w1, None), (W1T, ("s", sl, "a"))],
                       [(W1T, ("s", sl, "a"))], f"{uid}_w1a{g2}_{ht}")
                swload(AP(W1T.t, 2048 * sl + 1024, [[4096, 128], [128, 8], [1, hw]]),
                       f_w1.t.ap()[li, :, 2730 + hs2:2730 + hs2 + hw]
                       .rearrange("(dc p) m -> p dc m", p=128),
                       [(f_w1, None), (W1T, ("s", sl, "g"))],
                       [(W1T, ("s", sl, "g"))], f"{uid}_w1g{g2}_{ht}")
                for (part, pbase) in (("a", 0), ("g", 1024)):
                    for dc in range(8):
                        for nn in range(2):
                            em.op("pe", lambda te, sl=sl, part=part, pbase=pbase,
                                  dc=dc, nn=nn, hw=hw: te.matmul(
                                PS_O[0:hw, pbase + 512 * nn:pbase + 512 * (nn + 1)],
                                AP(W1T.t, 2048 * sl + (0 if part == "a" else 1024)
                                   + 128 * dc, [[4096, 128], [1, hw]]),
                                AP(XNFG.t, xnf_col(dc) + 512 * nn,
                                   [[8196, 128], [1, 512]]),
                                start=(dc == 0), stop=(dc == 7)),
                                [(W1T, ("s", sl, part)), (XNFG, None)],
                                [(PS_O, ("w1", part, nn))],
                                tag=f"{uid}_w1m{g2}_{ht}_{part}_{dc}_{nn}")
                em.op("act", lambda a, hw=hw, ht=ht: a.activation(
                    TMP[0:hw, 0:1024], PS_O[0:hw, 1024:2048], AF.Gelu,
                    bias=B1C[0:hw, 22 + ht:23 + ht]),
                    [(PS_O, ("w1", "g", 0)), (PS_O, ("w1", "g", 1)), (B1C, None)],
                    [(TMP, R(0, 1024))], tag=f"{uid}_gel{g2}_{ht}")
                em.op("dve", lambda v, hw=hw, ht=ht: v.scalar_tensor_tensor(
                    H2F[0:hw, 1024 * ht:1024 * (ht + 1)], PS_O[0:hw, 0:1024],
                    B1C[0:hw, ht:ht + 1], TMP[0:hw, 0:1024], ALU.add, ALU.mult),
                    [(PS_O, ("w1", "a", 0)), (PS_O, ("w1", "a", 1)),
                     (TMP, R(0, 1024)), (B1C, None)],
                    [(H2F, R(1024 * ht, 1024 * (ht + 1)))],
                    tag=f"{uid}_h2{g2}_{ht}")
            if fsect < 4:
                continue
            for ot in range(8):
                w2b = W2T if ot % 2 == 0 else W2TB
                swload(AP(w2b.t, 0, [[2816, 128], [128, 21], [1, 128]]),
                       f_w2.t.ap()[li, 0:2688, 128 * ot:128 * (ot + 1)]
                       .rearrange("(ht p) m -> p ht m", p=128),
                       [(f_w2, None), (w2b, None)],
                       [(w2b, None)], f"{uid}_w2{g2}_{ot}")
                swload(AP(w2b.t, 21 * 128, [[2816, 42], [1, 128]]),
                       f_w2.t.ap()[li, 2688:2730, 128 * ot:128 * (ot + 1)],
                       [(f_w2, None), (w2b, None)],
                       [(w2b, None)], f"{uid}_w2r{g2}_{ot}")
                for nn in range(2):
                    pb = PS_A if nn == 0 else PS_B
                    for ht in range(22):
                        hw = min(128, 2730 - 128 * ht)
                        em.op("pe", lambda te, ht=ht, nn=nn, hw=hw, pb=pb,
                              w2b=w2b: te.matmul(
                            pb[:, 0:512],
                            AP(w2b.t, 128 * ht, [[2816, hw], [1, 128]]),
                            AP(WQKV.t, 1024 * ht + 512 * nn,
                               [[24576, hw], [1, 512]]),
                            start=(ht == 0), stop=(ht == 21)),
                            [(w2b, None), (H2F, R(1024 * ht, 1024 * (ht + 1)))],
                            [(pb, R(0, 512))],
                            tag=f"{uid}_w2m{g2}_{ot}_{nn}_{ht}")
                    em.op("act", lambda a, ot=ot, nn=nn, pb=pb: a.activation(
                        OUT2F[:, 1024 * ot + 512 * nn:1024 * ot + 512 * (nn + 1)],
                        pb[:, 0:512], AF.Identity, bias=B2C[:, ot:ot + 1]),
                        [(pb, R(0, 512)), (B2C, None)],
                        [(OUT2F, ("o2", ot, nn))], tag=f"{uid}_w2e{g2}_{ot}_{nn}")
            for tl in range(8):
                t = 8 * g2 + tl
                for bh in range(2):
                    for cc in range(4):
                        ot = 4 * bh + cc
                        em.op("pe", lambda te, ot=ot, tl=tl, cc=cc: te.matmul(
                            PS_T[:, 128 * cc:128 * (cc + 1)],
                            OUT2F[:, 1024 * ot + 128 * tl:1024 * ot + 128 * (tl + 1)],
                            IDN.ap, start=True, stop=True),
                            [(OUT2F, ("o2", ot, 0)), (OUT2F, ("o2", ot, 1)),
                             (IDN, None)],
                            [(PS_T, R(128 * cc, 128 * (cc + 1)))],
                            tag=f"{uid}_oT{g2}_{tl}_{ot}")
                    em.op("dve", lambda v, t=t, bh=bh: v.tensor_add(
                        XO[:, 1024 * t + 512 * bh:1024 * t + 512 * (bh + 1)],
                        XO[:, 1024 * t + 512 * bh:1024 * t + 512 * (bh + 1)],
                        PS_T[:, 0:512]),
                        [(PS_T, R(0, 512)), (XO, ("t", t))], [(XO, ("t", t))],
                        tag=f"{uid}_ores{g2}_{tl}_{bh}")

    def emit_ffn_block(callid, nlayers=6, skip_last_ffn=False):
        swload(MSKG[:, 0:1152], msks_in.t.ap(),
               [(msks_in, None), (MSKG, None)], [(MSKG, None)],
               f"FB{callid}_msks")
        emit_ffn_entry()
        for li in range(nlayers):
            emit_attn_sub(li, f"F{callid}a{li}")
            if skip_last_ffn and li == nlayers - 1:
                break
            emit_ffn_sub(li, f"F{callid}f{li}")
        emit_ffn_exit(callid)



    nl = int(os.environ.get("KNL", "6"))
    if not os.environ.get("KSKA"):
        emit_attn_a(None)
    if stage >= 2:
        emit_ffn_block(0, nl, skip_last_ffn=bool(os.environ.get("KSKF")))
    if stage >= 3:
        emit_attn_b()
    if stage >= 4:
        emit_ffn_block(1, nl)


    for t in range(NT):
        hwload(out.t.ap().rearrange("(t p) d -> p t d", p=128)[:, t],
               XO[:, 1024 * t:1024 * (t + 1)],
               [(XO, ("t", t))], [(out, ("t", t))], f"out{t}")

    em.emit()
    return nc



def _rotary_tables(hf):
    j = np.arange(32, dtype=np.float64)
    inv_freq = 1.0 / 10000.0 ** (2.0 * j / 64.0)
    t = np.arange(2048 * hf, 2048 * (hf + 1), dtype=np.float64)
    freqs = t[:, None] * inv_freq[None, :]
    base = (2.0 * j + 0.4 * 64.0) / (1.4 * 64.0)
    power = (t - 2048.0) / 128.0
    scale = base[None, :] ** power[:, None]
    cs_q = np.cos(freqs) * scale
    ss_q = np.sin(freqs) * scale
    cs_k = np.cos(freqs) / scale
    ss_k = np.sin(freqs) / scale
    return np.stack([cs_q, ss_q, cs_k, ss_k]).astype(np.float32)


def host_constants(core):
    hf = core % 2
    c = {}
    c["ident"] = np.eye(128, dtype=np.float32)
    c["identf"] = np.eye(128, dtype=np.float32)
    A = np.zeros((16, 128, 512), dtype=np.float32)
    Bm = np.zeros((16, 128, 512), dtype=np.float32)
    W = np.zeros((128, 16), dtype=np.float32)
    cols = np.arange(512)[None, :]
    rows = np.arange(128)[:, None]
    for j in range(16):
        g = 16 * hf + j
        off = 128 * (g % 4)
        trij = np.where(cols > rows + off, NEG, 0.0).astype(np.float32)
        if hf == 0:
            A[j] = trij
            Bm[j] = NEG
            W[:, j] = -1e8
        else:
            A[j] = 0.0
            Bm[j] = trij
            W[:, j] = 0.0
    c["mska"] = A
    c["mskb"] = Bm
    c["wcol"] = W
    sel = np.zeros((8, 16, 128), dtype=np.float32)
    for p in range(8):
        sel[p, 2 * p, 0:64] = 1.0
        sel[p, 2 * p + 1, 64:128] = 1.0
    c["selp"] = sel
    c["tbl"] = _rotary_tables(hf)
    k = np.arange(128)[:, None]
    m = np.arange(128)[None, :]
    c["shu"] = (k == m - 1).astype(np.float32)
    c["shd"] = (k == m + 1).astype(np.float32)
    e = np.zeros((128, 1), dtype=np.float32); e[127, 0] = 1.0
    c["e127"] = e
    s64 = np.zeros((128, 64), dtype=np.float32); s64[0, 63] = 1.0
    c["sel64"] = s64
    p = np.arange(128)[:, None]; cc2 = np.arange(128)[None, :]
    tria = np.where(cc2 < p, NEG, 0.0).astype(np.float32)
    tric = np.where(cc2 > p, NEG, 0.0).astype(np.float32)
    c["tria"] = tria
    c["tric"] = tric
    c["lma"] = np.full((128, 128), NEG, dtype=np.float32) if hf == 0 else tria
    c["lm0b"] = (np.full((128, 128), NEG, dtype=np.float32) if hf == 0
                 else np.zeros((128, 128), dtype=np.float32))
    c["hsc"] = np.array([[0.0 if hf == 0 else 1.0]], dtype=np.float32)
    msks = np.zeros((128, 1152), dtype=np.float32)
    msks[:, 0:128] = c["lma"]
    msks[:, 128:256] = c["lm0b"]
    msks[:, 256:384] = tric
    msks[:, 384:512] = tria
    msks[:, 640:768] = tric
    msks[:, 768:896] = c["lma"]
    msks[:, 1024:1152] = tric
    c["msks"] = msks
    qk = 1.0 / (np.float64(np.sqrt(np.float32(1024))) + 1e-8)
    pp = np.arange(128, dtype=np.float64)
    uu = np.arange(512, dtype=np.float64)
    c["dcr"] = np.concatenate([
        np.exp(-1e-4 * pp), np.exp(1e-4 * pp),
        np.exp(1e-4 * uu), np.exp(-1e-4 * uu)])[None, :].astype(np.float32)
    dsab = np.zeros((128, 256), dtype=np.float64)
    for qt in range(16):
        i0 = 2048 * hf + 128 * qt
        for ci in range(8):
            delta = np.float64(i0 - 512 * ci)
            dsab[:, 8 * qt + ci] = qk * np.exp(-1e-4 * delta)
            dsab[:, 128 + 8 * qt + ci] = qk * np.exp(1e-4 * delta)
    c["dsab"] = dsab.astype(np.float32)
    return c


_nc_cache = {}


def kernel(L0_attn_a_in_features, L0_attn_a_q_proj_weight,
           L0_attn_a_k_proj_weight, L0_attn_a_v_proj_weight,
           L0_attn_a_o_proj_weight, L0_attn_b_Wq, L0_attn_b_Wk, L0_attn_b_Wv,
           L0_attn_b_Wo, L0_attn_b_W_down_kv, L0_attn_b_W_up_k,
           L0_attn_b_W_up_v, L0_ffn_Wqkv, L0_ffn_q_scale, L0_ffn_k_scale,
           L0_ffn_Wo, L0_ffn_ln_g, L0_ffn_ln_b, L0_ffn_W1, L0_ffn_b1,
           L0_ffn_W2, L0_ffn_b2):
    x = np.ascontiguousarray(L0_attn_a_in_features)
    if "nc" not in _nc_cache:
        _nc_cache["nc"] = build_program(int(os.environ.get("KST", str(STAGE))))
    nc = _nc_cache["nc"]
    in_maps = make_inmaps(
        x, L0_attn_a_q_proj_weight, L0_attn_a_k_proj_weight,
        L0_attn_a_v_proj_weight, L0_attn_a_o_proj_weight, L0_ffn_Wqkv,
        L0_ffn_q_scale, L0_ffn_k_scale, L0_ffn_Wo, L0_ffn_ln_g, L0_ffn_ln_b,
        L0_ffn_W1, L0_ffn_b1, L0_ffn_W2, L0_ffn_b2,
        L0_attn_b_Wq, L0_attn_b_Wk, L0_attn_b_Wv, L0_attn_b_Wo,
        L0_attn_b_W_down_kv, L0_attn_b_W_up_k, L0_attn_b_W_up_v)
    res = run_bass_kernel_spmd(nc, in_maps, core_ids=list(range(8)))
    outp = np.zeros((B, S, D), dtype=np.float32)
    for core in range(8):
        b, hf = core // 2, core % 2
        outp[b, 2048 * hf:2048 * (hf + 1), :] = res.results[core]["out"]
    return outp


def make_inmaps(x, L0_attn_a_q_proj_weight, L0_attn_a_k_proj_weight,
                L0_attn_a_v_proj_weight, L0_attn_a_o_proj_weight, L0_ffn_Wqkv,
                L0_ffn_q_scale, L0_ffn_k_scale, L0_ffn_Wo, L0_ffn_ln_g,
                L0_ffn_ln_b, L0_ffn_W1, L0_ffn_b1, L0_ffn_W2, L0_ffn_b2,
                L0_attn_b_Wq=None, L0_attn_b_Wk=None, L0_attn_b_Wv=None,
                L0_attn_b_Wo=None, L0_attn_b_W_down_kv=None,
                L0_attn_b_W_up_k=None, L0_attn_b_W_up_v=None):
    in_maps = []
    for core in range(8):
        b, hf = core // 2, core % 2
        m = host_constants(core)
        m["x0"] = np.ascontiguousarray(x[b, 2048 * hf:2048 * (hf + 1), :])
        m["wq_a"] = L0_attn_a_q_proj_weight
        m["wk_a"] = L0_attn_a_k_proj_weight
        m["wv_a"] = L0_attn_a_v_proj_weight
        m["wo_a"] = L0_attn_a_o_proj_weight
        if L0_attn_b_Wq is not None:
            m["wq_b"] = L0_attn_b_Wq
            m["wk_b"] = L0_attn_b_Wk
            m["wv_b"] = L0_attn_b_Wv
            m["wo_b"] = L0_attn_b_Wo
            m["wdn_b"] = L0_attn_b_W_down_kv
            m["wupk_b"] = L0_attn_b_W_up_k
            m["wupv_b"] = L0_attn_b_W_up_v
        m["f_wqkv"] = L0_ffn_Wqkv
        m["f_qs"] = L0_ffn_q_scale
        m["f_ks"] = L0_ffn_k_scale
        m["f_wo"] = L0_ffn_Wo
        m["f_lng"] = np.ascontiguousarray(
            L0_ffn_ln_g.reshape(6, 8, 128).transpose(0, 2, 1))
        m["f_lnb"] = np.ascontiguousarray(
            L0_ffn_ln_b.reshape(6, 8, 128).transpose(0, 2, 1))
        b1r = np.zeros((6, 128, 44), dtype=np.float32)
        b1a = L0_ffn_b1[:, 0:2730]
        b1g = L0_ffn_b1[:, 2730:5460]
        b1r[:, :, 0:21] = b1a[:, 0:2688].reshape(6, 21, 128).transpose(0, 2, 1)
        b1r[:, 0:42, 21] = b1a[:, 2688:2730]
        b1r[:, :, 22:43] = b1g[:, 0:2688].reshape(6, 21, 128).transpose(0, 2, 1)
        b1r[:, 0:42, 43] = b1g[:, 2688:2730]
        m["f_b1"] = b1r
        m["f_w1"] = L0_ffn_W1
        m["f_w2"] = L0_ffn_W2
        m["f_b2"] = np.ascontiguousarray(
            L0_ffn_b2.reshape(6, 8, 128).transpose(0, 2, 1))
        in_maps.append(m)
    return in_maps
